# version 11
# baseline (speedup 1.0000x reference)
import numpy as np
import ml_dtypes
from contextlib import ExitStack

import concourse.bass as bass
import concourse.mybir as mybir
from concourse.bass_utils import run_bass_kernel_spmd

F32 = mybir.dt.float32
BF16 = mybir.dt.bfloat16
U8 = mybir.dt.uint8
AF = mybir.ActivationFunctionType
ALU = mybir.AluOpType
AX = mybir.AxisListType

B, S, E = 4, 4096, 1024
H, D, WIN = 16, 64, 512
TOK = 2048
HALO = 512
TOKH = TOK + HALO
NCH = TOKH // 128
NQT = 16
NDT = 5
C_STAB = 10.0
SLOPES = [2.0 ** (-(h + 1) * 0.5) for h in range(H)]
F1 = 4096
NEG = -1.0e30


class Sch:
    ENGS = ("sync", "gpsimd", "tensor", "vector", "scalar")

    def __init__(self, nc, stack):
        self.nc = nc
        self.stack = stack
        self.ops = {e: [] for e in self.ENGS}
        self.sems = {}
        self.cnt = {}

    def sem(self, name):
        h = self.stack.enter_context(self.nc.semaphore(name))
        self.sems[name] = h
        self.cnt[name] = 0

    def op(self, eng, fn, inc=None, k=1, waits=()):
        w = {}
        for s, v in waits:
            if v and v > 0:
                w[s] = max(w.get(s, 0), v)
        n = None
        if inc is not None:
            self.cnt[inc] += k
            n = self.cnt[inc]
        sems = self.sems

        def thunk(e, fn=fn, w=tuple(w.items()), inc=inc, k=k):
            for s, v in w:
                e.wait_ge(sems[s], v)
            i = fn(e)
            if inc is not None:
                i.then_inc(sems[inc], k)

        self.ops[eng].append(thunk)
        return n

    def dma(self, eng, out, in_, inc, waits=()):
        return self.op(eng, lambda e: e.dma_start(out, in_), inc=inc, k=16,
                       waits=waits)

    def wait(self, eng, *waits):
        w = [(s, v) for (s, v) in waits if v and v > 0]
        if not w:
            return
        sems = self.sems

        def thunk(e, w=tuple(w)):
            for s, v in w:
                e.wait_ge(sems[s], v)

        self.ops[eng].append(thunk)

    def phase_barrier(self, engs=None):
        snap = [(s, self.cnt[s]) for s in self.sems]
        for eng in (engs or self.ENGS):
            self.wait(eng, *snap)

    def flush(self, block):
        for eng in self.ENGS:
            ops = self.ops[eng]
            if not ops:
                continue
            self.ops[eng] = []

            def body(e, ops=ops):
                for f in ops:
                    f(e)
            getattr(block, eng)(body)


def build_program(tap=None, stop_after=None, att_heads=4, qkv_parts="qkv",
                  detect_races=True, ffn2_serial=False, ffn2_nocorr=False,
                  ffn_stage_barriers=True):
    nc = bass.Bass("TRN2", target_bir_lowering=False,
                   detect_race_conditions=detect_races)
    stack = ExitStack()

    xc = nc.dram_tensor("xc", [TOKH, E], F32, kind="ExternalInput").ap()
    wg = nc.dram_tensor("wg", [4, 8, 128, 768], F32, kind="ExternalInput").ap()
    wo = nc.dram_tensor("wo", [8, 128, 1024], F32, kind="ExternalInput").ap()
    k0 = nc.dram_tensor("k0", [16, 128, F1], U8, kind="ExternalInput").ap()
    k1r = nc.dram_tensor("k1r", [32, 128, 32, 128], U8, kind="ExternalInput").ap()
    k2r = nc.dram_tensor("k2r", [32, 128, 1024], U8, kind="ExternalInput").ap()
    svec = nc.dram_tensor("svec", [128, 3], F32, kind="ExternalInput").ap()
    abias = nc.dram_tensor("abias", [H, 5, NDT, 128, 128], F32,
                           kind="ExternalInput").ap()
    eye = nc.dram_tensor("eye", [128, 128], BF16, kind="ExternalInput").ap()
    out = nc.dram_tensor("out", [TOK, E], F32, kind="ExternalOutput").ap()

    w0s = nc.dram_tensor("w0s", [32, 128, 1024], BF16, kind="Internal").ap()
    xns = nc.dram_tensor("xns", [8, 128, TOK], BF16, kind="Internal").ap()
    accs = nc.dram_tensor("accs", [TOK, E], F32, kind="Internal").ap()
    dend = nc.dram_tensor("dend", [H, NQT, 128], F32, kind="Internal").ap()

    tap_ap = None
    if tap == "xn":
        tap_ap = nc.dram_tensor("tap", [8, 128, TOKH], BF16, kind="ExternalOutput").ap()
    elif tap == "q":
        tap_ap = nc.dram_tensor("tap", [128, 2, TOK], BF16, kind="ExternalOutput").ap()
    elif tap == "k":
        tap_ap = nc.dram_tensor("tap", [128, 2, TOKH], BF16, kind="ExternalOutput").ap()
    elif tap == "v":
        tap_ap = nc.dram_tensor("tap", [128, NCH * 4 * 65], BF16, kind="ExternalOutput").ap()
    elif tap == "attn":
        tap_ap = nc.dram_tensor("tap", [128, 8, TOK], BF16, kind="ExternalOutput").ap()
    elif tap == "acc":
        tap_ap = nc.dram_tensor("tap", [TOK, E], F32, kind="ExternalOutput").ap()
    elif tap in ("h1", "h2"):
        tap_ap = nc.dram_tensor("tap", [128, 32, 512], BF16, kind="ExternalOutput").ap()

    sch = Sch(nc, stack)
    for s in ("SX", "SG", "SP", "SA", "SV", "SPL", "SO"):
        sch.sem(s)

    sb = lambda *a: stack.enter_context(nc.sbuf_tensor(*a))

    eye_sb = sb("eye_sb", [128, 128], BF16)
    svec_sb = sb("svec_sb", [128, 3], F32)
    scl = sb("scl", [128, 3], F32)
    ones_pe = sb("ones_pe", [128, 1], BF16)
    oneq = sb("oneq", [1, 128], F32)
    negh = sb("negh", [1, 512], F32)
    negh_bf = sb("negh_bf", [1, 512], BF16)
    rs1_sb = sb("rs1_sb", [1, 512], F32)
    rs2_sb = sb("rs2_sb", [1, 512], F32)
    rs1_hi = sb("rs1_hi", [1, 512], BF16)
    rs1_lo = sb("rs1_lo", [1, 512], BF16)
    rs2_hi = sb("rs2_hi", [1, 512], BF16)
    rs2_lo = sb("rs2_lo", [1, 512], BF16)
    at = sb("at", [128, 2 * 1024], F32)
    ob = sb("ob", [128, 2 * 1024], F32)
    fen_v = sb("fen_v", [128, 8], F32)
    fen_a = sb("fen_a", [128, 8], F32)
    fen_g = sb("fen_g", [128, 8], BF16)

    with nc.Block() as block:
        p0 = ExitStack()
        sb0 = lambda *a: p0.enter_context(nc.sbuf_tensor(*a))
        xn_stack = ExitStack()
        xn = xn_stack.enter_context(nc.sbuf_tensor("xn", [128, 8 * TOKH], BF16))

        def xn_v(ec, t0, n):
            return xn[:, ec * TOKH + t0: ec * TOKH + t0 + n]

        xtok = sb0("xtok", [128, 2 * 1024], F32)
        xnt = sb0("xnt", [128, 2 * 1024], BF16)
        sq = sb0("sq", [128, 1024], F32)
        xsum = sb0("xsum", [128, NCH], F32)
        xsqs = sb0("xsqs", [128, NCH], F32)
        tmc = sb0("tmc", [128, 2], F32)
        tve = sb0("tve", [128, 2], F32)
        tms = sb0("tms", [128, 2], F32)
        tsd = sb0("tsd", [128, 2], F32)
        trs = sb0("trs", [128, 2], F32)
        k0t = sb0("k0t", [128, 2 * F1], BF16)
        k0b = sb0("k0b", [128, 2 * F1], BF16)
        w0c = sb0("w0c", [128, 2 * F1], BF16)
        pst = p0.enter_context(nc.psum_tensor("pst", [128, 2048], BF16))

        sx_eye = sch.dma("sync", eye_sb[:], eye, "SX")
        sx_sv = sch.dma("sync", svec_sb[:], svec, "SX")
        sch.op("gpsimd", lambda e: e.memset(ones_pe[:], 1.0))
        sch.op("gpsimd", lambda e: e.memset(oneq[:], 1.0))
        sch.op("gpsimd", lambda e: e.memset(negh[:], -0.5))
        sch.op("gpsimd", lambda e: e.memset(negh_bf[:], -0.5))
        spl_ms = sch.op("gpsimd", lambda e: e.memset(xsqs[:], 0.0), inc="SPL")
        spl_init = spl_ms
        sv_scl = None
        for j, f in ((0, 2.0), (1, 4.0), (2, 4.0)):
            sv_scl = sch.op(
                "vector",
                lambda e, j=j, f=f: e.tensor_scalar_mul(
                    scl[:, j:j + 1], svec_sb[:, j:j + 1], f),
                inc="SV", waits=[("SX", sx_sv)])

        sv_stt = [0] * NCH
        sa_sq = [0] * NCH
        sp_tr = [0] * NCH
        sa_ev = [0] * NCH
        c1 = 1.0 / 1024.0

        spl_sub = [0] * 8
        sx_w0 = [0] * 8
        sg_k0 = [0] * 8

        def emit_k0(ec):
            sl = ec % 2
            sch.dma("gpsimd", k0t[:, sl * F1:(sl + 1) * F1], k0[ec], "SG",
                    waits=[("SPL", spl_sub[ec - 2] if ec >= 2 else 0)])
            sg_k0[ec] = sch.dma("gpsimd", k0b[:, sl * F1:(sl + 1) * F1],
                                k0[ec + 8], "SG")
            spl_sub[ec] = sch.op(
                "gpsimd",
                lambda e, sl=sl: e.tensor_sub(w0c[:, sl * F1:(sl + 1) * F1],
                                              k0t[:, sl * F1:(sl + 1) * F1],
                                              k0b[:, sl * F1:(sl + 1) * F1]),
                inc="SPL",
                waits=[("SG", sg_k0[ec]), ("SX", sx_w0[ec - 2] if ec >= 2 else 0)])
            spl_sub[ec] = sch.op(
                "gpsimd",
                lambda e, sl=sl: e.tensor_copy(fen_g[:],
                                               w0c[:, sl * F1: sl * F1 + 8]),
                inc="SPL", waits=[("SPL", spl_sub[ec])])
            for f1c in range(32):
                sx_w0[ec] = sch.dma(
                    "sync", w0s[f1c][:, ec * 128:(ec + 1) * 128],
                    w0c[:, sl * F1 + f1c * 128: sl * F1 + (f1c + 1) * 128], "SX",
                    waits=([("SPL", spl_sub[ec])] if f1c == 0 else []))

        for tc in range(NCH):
            sl = tc % 2
            sx_x = sch.dma(
                "sync", xtok[:, sl * 1024:(sl + 1) * 1024],
                xc[tc * 128:(tc + 1) * 128, :], "SX",
                waits=[("SV", sv_stt[tc - 2] if tc >= 2 else 0),
                       ("SA", sa_sq[tc - 2] if tc >= 2 else 0)])
            xt = xtok[:, sl * 1024:(sl + 1) * 1024]
            sch.op("vector",
                   lambda e, xt=xt, tc=tc: e.reduce_sum(
                       xsum[:, tc:tc + 1], xt, axis=AX.X),
                   inc="SV", waits=[("SX", sx_x)])
            sa_sq[tc] = sch.op(
                "scalar",
                lambda e, xt=xt, tc=tc: e.activation(
                    sq[:], xt, AF.Square, accum_out=xsqs[:, tc:tc + 1]),
                inc="SA", waits=[("SX", sx_x), ("SPL", spl_ms)])
            sa_sq[tc] = sch.op(
                "scalar",
                lambda e, tc=tc: e.copy(fen_a[0:1, 0:1], xsqs[0:1, tc:tc + 1]),
                inc="SA", waits=[("SA", sa_sq[tc])])
            sch.op("vector",
                   lambda e, tc=tc, sl=sl: e.tensor_scalar(
                       tve[:, sl:sl + 1], xsqs[:, tc:tc + 1], c1, 1e-5,
                       ALU.mult, ALU.add),
                   inc="SV", waits=[("SA", sa_sq[tc])])
            sv_m = sch.op("vector",
                          lambda e, tc=tc, sl=sl: e.tensor_scalar_mul(
                              tmc[:, sl:sl + 1], xsum[:, tc:tc + 1], c1),
                          inc="SV")
            sv_ms = sch.op("vector",
                           lambda e, sl=sl: e.tensor_mul(
                               tms[:, sl:sl + 1], tmc[:, sl:sl + 1],
                               tmc[:, sl:sl + 1]),
                           inc="SV", waits=[("SV", sv_m)])
            sv_ve = sch.op("vector",
                           lambda e, sl=sl: e.tensor_sub(
                               tve[:, sl:sl + 1], tve[:, sl:sl + 1],
                               tms[:, sl:sl + 1]), inc="SV",
                           waits=[("SV", sv_ms)])
            sa_sd = sch.op("scalar",
                           lambda e, sl=sl: e.sqrt(tsd[:, sl:sl + 1],
                                                   tve[:, sl:sl + 1]),
                           inc="SA", waits=[("SV", sv_ve)])
            sa_sd = sch.op("scalar",
                           lambda e, sl=sl: e.copy(fen_a[0:1, 0:1],
                                                   tsd[0:1, sl:sl + 1]),
                           inc="SA", waits=[("SA", sa_sd)])
            sv_rc = sch.op("vector",
                           lambda e, sl=sl: e.reciprocal(trs[:, sl:sl + 1],
                                                         tsd[:, sl:sl + 1]),
                           inc="SV", waits=[("SA", sa_sd)])
            xnv = xnt[:, sl * 1024:(sl + 1) * 1024]
            sv_stt[tc] = sch.op(
                "vector",
                lambda e, xt=xt, xnv=xnv, sl=sl: e.tensor_scalar(
                    xnv, xt, tmc[:, sl:sl + 1], trs[:, sl:sl + 1],
                    ALU.subtract, ALU.mult),
                inc="SV", waits=[("SP", sp_tr[tc - 2] if tc >= 2 else 0),
                                 ("SV", sv_rc)])
            sv_stt[tc] = sch.op(
                "vector",
                lambda e, xnv=xnv: e.tensor_copy(fen_v[:], xnv[:, 0:8]),
                inc="SV", waits=[("SV", sv_stt[tc])])
            pslot = pst[:, (tc % 2) * 1024:(tc % 2) * 1024 + 1024]
            for j in range(8):
                sp_tr[tc] = sch.op(
                    "tensor",
                    lambda e, xnv=xnv, j=j, pslot=pslot: e.transpose(
                        pslot[:, j * 128:(j + 1) * 128],
                        xnv[:, j * 128:(j + 1) * 128], eye_sb[:]),
                    inc="SP",
                    waits=([("SV", sv_stt[tc]), ("SX", sx_eye),
                            ("SA", sa_ev[tc - 2] if tc >= 2 else 0)]
                           if j == 0 else []))
            xn_dst = xn[:].rearrange("p (c t) -> p c t", c=8)[
                :, :, tc * 128:(tc + 1) * 128]
            pst_v = pslot.rearrange("p (c t) -> p c t", c=8)
            sa_ev[tc] = sch.op(
                "scalar",
                lambda e, xn_dst=xn_dst, pst_v=pst_v: e.copy(xn_dst, pst_v),
                inc="SA", waits=[("SP", sp_tr[tc])])
            if tc >= 4 and tc % 2 == 0 and tc // 2 - 2 < 8:
                emit_k0(tc // 2 - 2)

        sa_fence = sch.op("scalar",
                          lambda e: e.copy(fen_a[:], xn[:, 0:8]),
                          inc="SA", waits=[("SA", sa_ev[NCH - 1])])
        sx_xns = 0
        for ec in range(8):
            sx_xns = sch.dma("sync", xns[ec], xn_v(ec, HALO, TOK), "SX",
                             waits=[("SA", sa_fence)])
        if tap == "xn":
            for ec in range(8):
                sch.dma("sync", tap_ap[ec], xn_v(ec, 0, TOKH), "SO",
                        waits=[("SA", sa_ev[NCH - 1])])

        if stop_after == "p0":
            sch.wait("sync", ("SO", sch.cnt["SO"]), ("SX", sch.cnt["SX"]))
            sch.flush(block)
            p0.close()
            xn_stack.close()
            stack.close()
            return nc
        sch.phase_barrier()
        sch.flush(block)
        p0.close()

        pb = ExitStack()
        sbB = lambda *a: pb.enter_context(nc.sbuf_tensor(*a))
        psB_ = lambda *a: pb.enter_context(nc.psum_tensor(*a))
        q_sb = sbB("q_sb", [128, 2, TOK], BF16)
        k_sb = sbB("k_sb", [128, 2, TOKH], BF16)
        v_sb = sbB("v_sb", [128, NCH, 4, 65], BF16)
        o_sb = sbB("o_sb", [128, 8, TOK], BF16)
        wgb = sbB("wgb", [128, 8, 768], BF16)
        ab_sb = sbB("ab_sb", [128, 5, NDT * 128], F32)
        p_sb = sbB("p_sb", [128, 3, NDT * 128], BF16)
        den_s = sbB("den_s", [1, 2 * 128], F32)
        den_all = sbB("den_all", [128, 256], F32)
        db_row = sbB("db_row", [1, 2 * 2048], F32)
        Dsb = sbB("Dsb", [128, 2 * 1024], F32)
        psS3 = psB_("psS3", [128, 3072], F32)
        psPV = psB_("psPV", [128, 1024], F32)

        sg_wg = [0] * 4
        sp_qkv = [0] * 4
        sp_att = [0] * 4
        sv_att = [0] * 4
        sa_att = [0] * 4
        it = 0
        sa_exp = {}
        sp_pv = {}
        sp_sc = {}
        sa_oev = {}
        sx_den = {}
        sp_nb = {}
        sa_nb = {}
        sv_nb = {}
        sv_ab = {}
        spl_ab = {}
        sv_vev = [0] * 4
        sa_qk = [0] * 4

        for g in range(4):
            for ec in range(8):
                n = sch.dma(
                    "gpsimd", wgb[:, ec, :], wg[g, ec], "SG",
                    waits=([("SP", sp_att[g - 1] if g >= 1 else 0)]
                           if ec == 0 else []))
            sg_wg[g] = n
            spl_v1 = sch.op(
                "gpsimd", lambda e: e.memset(v_sb[:, :, :, 64:65], 1.0),
                inc="SPL", waits=[("SP", sp_att[g - 1] if g >= 1 else 0)])
            qkv_entry = [("SG", sg_wg[g])]
            pp = 0
            sa_last_evac = {0: 0, 1: 0}
            for mc in range(2):
                for what, nspan, cols in (("q", 4, 0), ("k", 5, 256)):
                    if what not in qkv_parts:
                        continue
                    for span in range(nspan):
                        pv = psS3[:, pp * 1024: pp * 1024 + 512]
                        t0 = (HALO if what == "q" else 0) + span * 512
                        ready = sa_ev[min(NCH - 1, t0 // 128 + 3)]
                        for ec in range(8):
                            w = (list(qkv_entry) + [("SA", sa_last_evac[pp]),
                                                    ("SA", ready)]
                                 if ec == 0 else [])
                            n = sch.op(
                                "tensor",
                                lambda e, pv=pv, ec=ec, mc=mc, cols=cols, t0=t0:
                                e.matmul(
                                    pv,
                                    wgb[:, ec, cols + mc * 128: cols + (mc + 1) * 128],
                                    xn_v(ec, t0, 512),
                                    start=(ec == 0), stop=(ec == 7)),
                                inc="SP" if ec == 7 else None, waits=w)
                        sp_qkv[g] = n
                        if what == "q":
                            dst = q_sb[:, mc, t0 - HALO: t0 - HALO + 512]
                            scale = 0.125
                        else:
                            dst = k_sb[:, mc, t0: t0 + 512]
                            scale = 1.0
                        sa_qk[g] = sa_last_evac[pp] = sch.op(
                            "scalar",
                            lambda e, pv=pv, dst=dst, scale=scale:
                            e.activation(dst, pv, AF.Copy, scale=scale),
                            inc="SA", waits=[("SP", sp_qkv[g])])
                        pp ^= 1
            sa_qk[g] = sch.op(
                "scalar", lambda e: e.copy(fen_a[:], k_sb[:, 0, 0:8]),
                inc="SA", waits=[("SA", sa_qk[g])])
            vpp = 0
            sv_last_vev = {0: 0, 1: 0}
            nv_ch = NCH if "v" in qkv_parts else (2 if "w" in qkv_parts else 0)
            for tc in range(nv_ch):
                pv = psS3[:, 2048 + vpp * 512: 2048 + vpp * 512 + 256]
                for ec in range(8):
                    w = ([("SPL", spl_v1), ("SV", sv_last_vev[vpp]),
                          ("SA", sa_ev[tc])]
                         + (qkv_entry if tc == 0 else []) if ec == 0 else [])
                    n = sch.op(
                        "tensor",
                        lambda e, pv=pv, ec=ec, tc=tc: e.matmul(
                            pv, xn_v(ec, tc * 128, 128), wgb[:, ec, 512:768],
                            start=(ec == 0), stop=(ec == 7)),
                        inc="SP" if ec == 7 else None, waits=w)
                sp_qkv[g] = n
                pvv = pv.rearrange("p (h d) -> p h d", h=4)
                sv_vev[g] = sv_last_vev[vpp] = sch.op(
                    "vector",
                    lambda e, pvv=pvv, tc=tc: e.tensor_copy(
                        v_sb[:, tc, :, 0:64], pvv),
                    inc="SV", waits=[("SP", sp_qkv[g])])
                vpp ^= 1
            sv_vev[g] = sch.op(
                "vector",
                lambda e: e.tensor_copy(fen_v[:], v_sb[:, 0, 0, 0:8]),
                inc="SV", waits=[("SV", sv_vev[g])])

            def emit_pv(ctx):
                cit, chh, cpo, cjg, chead, cqt, cslot = ctx
                vslot = cit % 2
                pvs = psPV[:, vslot * 512: vslot * 512 + 128]
                for dt in range(NDT):
                    kc = cqt + dt
                    w = [("SA", sa_exp[cit])]
                    if dt == 0:
                        w.append(("SA", sa_oev.get(cit - 2, 0)))
                    sp_pv[cit] = sch.op(
                        "tensor",
                        lambda e, pvs=pvs, kc=kc, chh=chh, cslot=cslot, dt=dt:
                        e.matmul(
                            pvs[0:65, :], v_sb[:, kc, chh, :],
                            p_sb[:, cslot, dt * 128:(dt + 1) * 128],
                            start=(dt == 0), stop=(dt == NDT - 1)),
                        inc="SP" if dt == NDT - 1 else None, waits=w)
                sch.op(
                    "scalar",
                    lambda e, pvs=pvs, cpo=cpo, cjg=cjg, cqt=cqt: e.copy(
                        o_sb[cpo:cpo + 64, cjg, cqt * 128:(cqt + 1) * 128],
                        pvs[0:64, :]),
                    inc="SA", waits=[("SP", sp_pv[cit])])
                sa_oev[cit] = sch.op(
                    "scalar",
                    lambda e, pvs=pvs, vslot=vslot: e.copy(
                        den_s[0:1, vslot * 128:(vslot + 1) * 128],
                        pvs[64:65, :]),
                    inc="SA",
                    waits=[("SX", sx_den.get(cit - 2, 0))])
                sa_oev[cit] = sch.op(
                    "scalar", lambda e: e.copy(fen_a[0:1, :], den_s[0:1, 0:8]),
                    inc="SA", waits=[("SA", sa_oev[cit])])
                sx_den[cit] = sch.dma(
                    "sync", dend[chead, cqt],
                    den_s[0:1, vslot * 128:(vslot + 1) * 128], "SX",
                    waits=[("SA", sa_oev[cit])])

            sv_adds = {}

            def emit_exp(ctx):
                cit, chh, cpo, cjg, chead, cqt, cslot = ctx
                psSl = psS3[:, cslot * 1024: cslot * 1024 + 640]
                sa_exp[cit] = sch.op(
                    "scalar",
                    lambda e, psSl=psSl, cslot=cslot: e.activation(
                        p_sb[:, cslot, :], psSl, AF.Exp),
                    inc="SA",
                    waits=[("SV", sv_adds[cit]), ("SP", sp_pv.get(cit - 3, 0))])

            q_exp = []
            q_pv = []
            for hh in range(att_heads):
                head = 4 * g + hh
                po = 64 * (hh % 2)
                mc = hh // 2
                jg = 2 * g + mc
                sx_ab = 0
                for c in range(5):
                    sx_ab = sch.dma(
                        "sync",
                        ab_sb[:, c].rearrange("p (d q) -> p d q", d=NDT),
                        abias[head, c].rearrange("d p q -> p d q"), "SX",
                        waits=([("SV", sv_ab.get(head - 1, 0))]
                               if c == 0 else []))
                for qt in range(NQT):
                    cls = min(qt, 4)
                    slot = it % 3
                    psS = psS3[:, slot * 1024: slot * 1024 + 640]
                    for dt in range(NDT):
                        kc = qt + dt
                        w = []
                        if dt == 0:
                            w = [("SA", sa_exp.get(it - 3, 0)),
                                 ("SA", sa_qk[g]), ("SV", sv_vev[g])]
                        n = sch.op(
                            "tensor",
                            lambda e, psS=psS, dt=dt, po=po, mc=mc, kc=kc, qt=qt:
                            e.matmul(
                                psS[:, dt * 128:(dt + 1) * 128],
                                k_sb[po:po + 64, mc, kc * 128:(kc + 1) * 128],
                                q_sb[po:po + 64, mc, qt * 128:(qt + 1) * 128],
                                start=True, stop=True),
                            inc="SP" if dt == NDT - 1 else None, waits=w)
                    sp_sc[it] = n
                    sv_adds[it] = sch.op(
                        "vector",
                        lambda e, psS=psS, cls=cls: e.tensor_add(
                            psS, psS, ab_sb[:, cls]),
                        inc="SV",
                        waits=[("SP", sp_sc[it]), ("SX", sx_ab)])
                    sv_ab[head] = sv_adds[it]
                    ctx = (it, hh, po, jg, head, qt, slot)
                    q_exp.append(ctx)
                    if len(q_exp) > 1:
                        emit_exp(q_exp.pop(0))
                    q_pv.append(ctx)
                    if len(q_pv) > 2:
                        emit_pv(q_pv.pop(0))
                    it += 1
            for ctx in q_exp:
                emit_exp(ctx)
            while q_pv:
                emit_pv(q_pv.pop(0))
            sp_att[g] = sch.cnt["SP"]
            sv_att[g] = sch.cnt["SV"]
            sa_att[g] = sch.cnt["SA"]

        sch.op("scalar", lambda e: e.copy(fen_a[:], o_sb[:, 0, 0:8]),
               inc="SA", waits=[("SA", sch.cnt["SA"])])
        att_sa = sch.cnt["SA"]
        att_sx = sch.cnt["SX"]
        sx_da = sch.dma("sync", den_all[:],
                        dend[:].rearrange("h (a b) d -> (h a) (b d)", b=2), "SX",
                        waits=[("SX", att_sx)])
        sv_da = sch.op("vector", lambda e: e.reciprocal(den_all[:], den_all[:]),
                       inc="SV", waits=[("SX", sx_da)])
        sv_da = sch.op("vector", lambda e: e.tensor_copy(fen_v[:], den_all[:, 0:8]),
                       inc="SV", waits=[("SV", sv_da)])
        sx_dw = sch.dma("sync",
                        dend[:].rearrange("h (a b) d -> (h a) (b d)", b=2),
                        den_all[:], "SX", waits=[("SV", sv_da)])
        sx_dbr = {}
        for head in range(H if att_heads == 4 else 0):
            po = 64 * (head % 2)
            jg = head // 2
            hs = head % 2
            sx_dbr[head] = sch.dma(
                "sync", db_row[0:1, hs * 2048:(hs + 1) * 2048],
                dend[head].rearrange("q d -> (q d)"), "SX",
                waits=[("SX", sx_dw), ("SP", sp_nb.get(2 * (head - 2) + 1, 0))])
            for j in range(2):
                gq = head * 2 + j
                slot = gq % 2
                psN = psS3[:, slot * 1024:(slot + 1) * 1024]
                w = [("SX", sx_dbr[head]), ("SA", sa_nb.get(gq - 2, 0)),
                     ("SPL", spl_init if gq == 0 else 0), ("SA", att_sa)]
                for half in range(2):
                    sp_nb[gq] = sch.op(
                        "tensor",
                        lambda e, psN=psN, hs=hs, j=j, half=half: e.matmul(
                            psN[:, half * 512:(half + 1) * 512], oneq[:],
                            db_row[0:1, hs * 2048 + j * 1024 + half * 512:
                                   hs * 2048 + j * 1024 + (half + 1) * 512],
                            start=True, stop=True),
                        inc="SP" if half == 1 else None,
                        waits=(w if half == 0 else []))
                sa_nb[gq] = sch.op(
                    "scalar",
                    lambda e, psN=psN, slot=slot: e.copy(
                        Dsb[:, slot * 1024:(slot + 1) * 1024], psN),
                    inc="SA",
                    waits=[("SP", sp_nb[gq]), ("SV", sv_nb.get(gq - 2, 0))])
                sv_nb[gq] = sch.op(
                    "vector",
                    lambda e, po=po, jg=jg, j=j, slot=slot: e.tensor_mul(
                        o_sb[po:po + 64, jg, j * 1024:(j + 1) * 1024],
                        o_sb[po:po + 64, jg, j * 1024:(j + 1) * 1024],
                        Dsb[po:po + 64, slot * 1024:(slot + 1) * 1024]),
                    inc="SV", waits=[("SA", sa_nb[gq])])
        sch.op("vector", lambda e: e.tensor_copy(fen_v[:], o_sb[:, 0, 0:8]),
               inc="SV", waits=[("SV", sch.cnt["SV"])])
        sv_norm = sch.cnt["SV"]
        norm_sa = sch.cnt["SA"]

        if tap == "q":
            sch.wait("sync", ("SA", sch.cnt["SA"]))
            for mcx in range(2):
                sch.dma("sync", tap_ap[:, mcx, :], q_sb[:, mcx, :], "SO")
        if tap == "k":
            sch.wait("sync", ("SA", sch.cnt["SA"]))
            for mcx in range(2):
                sch.dma("sync", tap_ap[:, mcx, :], k_sb[:, mcx, :], "SO")
        if tap == "v":
            sch.wait("sync", ("SV", sch.cnt["SV"]))
            sch.dma("sync", tap_ap[:],
                    v_sb[:].rearrange("p c h d -> p (c h d)"), "SO")
        if stop_after == "qkv":
            sch.wait("sync", ("SO", sch.cnt["SO"]), ("SX", sch.cnt["SX"]),
                     ("SA", sch.cnt["SA"]), ("SV", sch.cnt["SV"]))
            sch.flush(block)
            pb.close()
            xn_stack.close()
            stack.close()
            return nc
        if tap == "attn":
            sch.wait("sync", ("SV", sv_norm))
            for jgx in range(8):
                sch.dma("sync", tap_ap[:, jgx, :], o_sb[:, jgx, :], "SO")

        wob = sbB("wob", [128, 8, 1024], BF16)
        sg_wo = 0
        for ec in range(8):
            sg_wo = sch.dma("gpsimd", wob[:, ec, :], wo[ec], "SG")
        sp_opA = {}
        sv_opadd = {}
        sx_acw = {}
        for tc in range(16):
            sl = tc % 2
            sx_opx = sch.dma(
                "sync", at[:, sl * 1024:(sl + 1) * 1024],
                xc[HALO + tc * 128: HALO + (tc + 1) * 128, :], "SX",
                waits=[("SV", sv_opadd.get(tc - 2, 0)),
                       ("SX", sx_acw.get(tc - 2, 0))])
            for half in range(2):
                pv = psS3[:, half * 512:(half + 1) * 512]
                for ec in range(8):
                    w = []
                    if ec == 0:
                        w = [("SV", sv_norm), ("SG", sg_wo),
                             ("SV", sv_opadd.get(tc - 1, 0)),
                             ("SA", norm_sa)]
                    sp_opA[(tc, half)] = sch.op(
                        "tensor",
                        lambda e, pv=pv, ec=ec, tc=tc, half=half: e.matmul(
                            pv, o_sb[:, ec, tc * 128:(tc + 1) * 128],
                            wob[:, ec, half * 512:(half + 1) * 512],
                            start=(ec == 0), stop=(ec == 7)),
                        inc="SP" if ec == 7 else None, waits=w)
                sv_opadd[tc] = sch.op(
                    "vector",
                    lambda e, pv=pv, sl=sl, half=half: e.tensor_add(
                        at[:, sl * 1024 + half * 512:
                           sl * 1024 + (half + 1) * 512],
                        pv,
                        at[:, sl * 1024 + half * 512:
                           sl * 1024 + (half + 1) * 512]),
                    inc="SV", waits=[("SP", sp_opA[(tc, half)]), ("SX", sx_opx)])
            sv_opadd[tc] = sch.op(
                "vector", lambda e: e.tensor_copy(fen_v[:], at[:, 0:8]),
                inc="SV", waits=[("SV", sv_opadd[tc])])
            sx_acw[tc] = sch.dma(
                "sync", accs[tc * 128:(tc + 1) * 128, :],
                at[:, sl * 1024:(sl + 1) * 1024], "SX",
                waits=[("SV", sv_opadd[tc])])
        if tap == "acc":
            sch.dma("sync", tap_ap[:], accs[:], "SO",
                    waits=[("SX", sx_acw[15])])

        if stop_after == "att":
            sch.wait("sync", ("SO", sch.cnt["SO"]), ("SX", sch.cnt["SX"]))
            sch.flush(block)
            pb.close()
            xn_stack.close()
            stack.close()
            return nc
        sch.phase_barrier()
        sch.flush(block)
        pb.close()
        xn_stack.close()

        pc = ExitStack()
        sbC = lambda *a: pc.enter_context(nc.sbuf_tensor(*a))
        psC = lambda *a: pc.enter_context(nc.psum_tensor(*a))
        h2 = sbC("h2", [128, 32 * 512], BF16)
        W2sb = sbC("W2sb", [128, 32 * 1024], BF16)
        h1 = sbC("h1", [128, 32 * 512], BF16)
        k1b = sbC("k1b", [128, 2 * 4096], BF16)
        w0b = sbC("w0b", [128, 2 * 1024], BF16)
        xnp = sbC("xnp", [128, 8 * 512], BF16)
        den_flush = sbC("den_flush", [1, 128], BF16)
        psF = psC("psF", [128, 1024], F32)
        psF3 = psC("psF3", [128, 1024], F32)
        psRS = psC("psRS", [128, 512], F32)

        sg_w2 = 0
        for f2c in range(32):
            sg_w2 = sch.dma("gpsimd", W2sb[:, f2c * 1024:(f2c + 1) * 1024],
                            k2r[f2c], "SG")

        sp_f1 = {}
        sa_r1 = {}
        sp_f2 = {}
        sa_r2 = {}
        sp_f3 = {}
        sv_f3 = {}
        sa_rs1 = sa_rs2 = 0
        so_out = 0
        prev_pass_sp = 0
        sx_xnp = 0

        for p in range(4):
            w = [("SX", sx_xns), ("SP", prev_pass_sp)]
            for ec in range(8):
                sx_xnp = sch.dma("sync", xnp[:, ec * 512:(ec + 1) * 512],
                                 xns[ec][:, p * 512:(p + 1) * 512], "SX",
                                 waits=(w if ec == 0 else []))
            for f1c in range(32):
                sl = f1c % 2
                sx_w0b = sch.dma(
                    "sync", w0b[:, sl * 1024:(sl + 1) * 1024], w0s[f1c], "SX",
                    waits=[("SX", sx_w0[7] if p == 0 and f1c < 2 else 0),
                           ("SP", sp_f1.get((p, f1c - 2), prev_pass_sp))])
                for ec in range(8):
                    w = []
                    if ec == 0:
                        w = [("SX", sx_w0b), ("SX", sx_xnp),
                             ("SA", sa_r1.get((p, f1c - 2), 0)),
                             ("SA", sa_rs2 if p > 0 else 0),
                             ("SV", sv_f3.get((p - 1, 3, 1), 0))]
                    sp_f1[(p, f1c)] = sch.op(
                        "tensor",
                        lambda e, sl=sl, ec=ec: e.matmul(
                            psF[:, sl * 512:(sl + 1) * 512],
                            w0b[:, sl * 1024 + ec * 128:
                                sl * 1024 + (ec + 1) * 128],
                            xnp[:, ec * 512:(ec + 1) * 512],
                            start=(ec == 0), stop=(ec == 7)),
                        inc="SP" if ec == 7 else None, waits=w)
                sa_r1[(p, f1c)] = sch.op(
                    "scalar",
                    lambda e, sl=sl, f1c=f1c: e.activation(
                        h1[:, f1c * 512:(f1c + 1) * 512],
                        psF[:, sl * 512:(sl + 1) * 512], AF.Relu,
                        scale=scl[:, 0:1]),
                    inc="SA", waits=[("SP", sp_f1[(p, f1c)]), ("SV", sv_scl)])
                for rc in ([f1c - 1] if f1c >= 1 else []) + ([31] if f1c == 31 else []):
                    if rc == 31:
                        fa = sch.op("scalar",
                                    lambda e: e.copy(fen_a[:],
                                                     h1[:, 31 * 512:31 * 512 + 8]),
                                    inc="SA", waits=[("SA", sa_r1[(p, 31)])])
                        w = [("SA", fa)]
                    else:
                        w = [("SA", sa_r1.get((p, rc + 1), sa_r1[(p, rc)]))]
                    if rc == 0:
                        w += [("SA", sa_rs1), ("SPL", spl_init)]
                    sp_rs1 = sch.op(
                        "tensor",
                        lambda e, rc=rc: e.matmul(
                            psRS[0:1, 0:512], ones_pe[:],
                            h1[:, rc * 512:(rc + 1) * 512],
                            start=(rc == 0), stop=(rc == 31)),
                        inc="SP" if rc == 31 else None, waits=w)
            if ffn_stage_barriers:
                sch.phase_barrier()
            sa_rs1 = sch.op("scalar",
                            lambda e: e.copy(rs1_sb[:], psRS[0:1, 0:512]),
                            inc="SA", waits=[("SP", sp_rs1)])
            sa_rs1 = sch.op("scalar",
                            lambda e: e.copy(rs1_hi[:], rs1_sb[:]),
                            inc="SA", waits=[("SA", sa_rs1)])
            sa_rs1 = sch.op("scalar",
                            lambda e: e.copy(fen_a[0:1, :], rs1_hi[0:1, 0:8]),
                            inc="SA", waits=[("SA", sa_rs1)])
            sv_rs1 = sch.op("vector",
                            lambda e: e.tensor_sub(rs1_lo[:], rs1_sb[:],
                                                   rs1_hi[:]),
                            inc="SV", waits=[("SA", sa_rs1)])
            sv_rs1 = sch.op("vector",
                            lambda e: e.tensor_copy(fen_v[0:1, :],
                                                    rs1_lo[0:1, 0:8]),
                            inc="SV", waits=[("SV", sv_rs1)])
            for f2c in range(32):
                sl = f2c % 2
                sg_k1 = sch.dma(
                    "gpsimd", k1b[:, sl * 4096:(sl + 1) * 4096],
                    k1r[f2c].rearrange("p c t -> p (c t)"), "SG",
                    waits=[("SP", sp_f2.get((p, f2c - 2),
                                            sp_f2.get((p - 1, 30 + f2c), 0)))])
                for f1c in range(34):
                    w = []
                    if f1c == 0:
                        w = [("SG", sg_k1), ("SA", sa_rs1), ("SV", sv_rs1),
                             ("SA", sa_r2.get((p, f2c - 2), 0))]
                        if ffn2_serial:
                            w.append(("SA", sa_r2.get((p, f2c - 1), 0)))
                            w.append(("SG", sch.cnt["SG"]))
                            w.append(("SX", sch.cnt["SX"]))
                    if f1c < 32:
                        sp_f2[(p, f2c)] = sch.op(
                            "tensor",
                            lambda e, sl=sl, f1c=f1c: e.matmul(
                                psF[:, sl * 512:(sl + 1) * 512],
                                k1b[:, sl * 4096 + f1c * 128:
                                    sl * 4096 + (f1c + 1) * 128],
                                h1[:, f1c * 512:(f1c + 1) * 512],
                                start=(f1c == 0),
                                stop=(ffn2_nocorr and f1c == 31)),
                            inc="SP" if (ffn2_nocorr and f1c == 31) else None,
                            waits=w)
                    elif not ffn2_nocorr:
                        rhs = rs1_hi if f1c == 32 else rs1_lo
                        sp_f2[(p, f2c)] = sch.op(
                            "tensor",
                            lambda e, sl=sl, rhs=rhs, f1c=f1c: e.matmul(
                                psF[:, sl * 512:(sl + 1) * 512],
                                negh_bf[0:1, 0:128], rhs[:],
                                start=False, stop=(f1c == 33)),
                            inc="SP" if f1c == 33 else None, waits=w)
                sa_r2[(p, f2c)] = sch.op(
                    "scalar",
                    lambda e, sl=sl, f2c=f2c: e.activation(
                        h2[:, f2c * 512:(f2c + 1) * 512],
                        psF[:, sl * 512:(sl + 1) * 512], AF.Relu,
                        scale=scl[:, 1:2]),
                    inc="SA", waits=[("SP", sp_f2[(p, f2c)])])
                for rc in ([f2c - 1] if f2c >= 1 else []) + ([31] if f2c == 31 else []):
                    if rc == 31:
                        fa = sch.op("scalar",
                                    lambda e: e.copy(fen_a[:],
                                                     h2[:, 31 * 512:31 * 512 + 8]),
                                    inc="SA", waits=[("SA", sa_r2[(p, 31)])])
                        w = [("SA", fa)]
                    else:
                        w = [("SA", sa_r2.get((p, rc + 1), sa_r2[(p, rc)]))]
                    if rc == 0:
                        w.append(("SA", sa_rs2))
                    sp_rs2 = sch.op(
                        "tensor",
                        lambda e, rc=rc: e.matmul(
                            psRS[0:1, 0:512], ones_pe[:],
                            h2[:, rc * 512:(rc + 1) * 512],
                            start=(rc == 0), stop=(rc == 31)),
                        inc="SP" if rc == 31 else None, waits=w)
            if ffn_stage_barriers:
                sch.phase_barrier()
            sa_rs2 = sch.op("scalar",
                            lambda e: e.copy(rs2_sb[:], psRS[0:1, 0:512]),
                            inc="SA", waits=[("SP", sp_rs2)])
            sa_rs2 = sch.op("scalar",
                            lambda e: e.copy(rs2_hi[:], rs2_sb[:]),
                            inc="SA", waits=[("SA", sa_rs2)])
            sa_rs2 = sch.op("scalar",
                            lambda e: e.copy(fen_a[0:1, :], rs2_hi[0:1, 0:8]),
                            inc="SA", waits=[("SA", sa_rs2)])
            sv_rs2 = sch.op("vector",
                            lambda e: e.tensor_sub(rs2_lo[:], rs2_sb[:],
                                                   rs2_hi[:]),
                            inc="SV", waits=[("SA", sa_rs2)])
            sv_rs2 = sch.op("vector",
                            lambda e: e.tensor_copy(fen_v[0:1, :],
                                                    rs2_lo[0:1, 0:8]),
                            inc="SV", waits=[("SV", sv_rs2)])
            if tap == "h1" and p == 0:
                sch.wait("sync", ("SA", sa_r1[(0, 31)]))
                sch.dma("sync", tap_ap[:].rearrange("p c t -> p (c t)"),
                        h1[:], "SO")
            if tap == "h2" and p == 0:
                sch.wait("sync", ("SA", sa_fl))
                sch.dma("sync", tap_ap[:].rearrange("p c t -> p (c t)"),
                        h2[:], "SO")
            if ffn_stage_barriers:
                sch.phase_barrier()
            for tc in range(4):
                sl = tc % 2
                sx_ac3 = sch.dma(
                    "sync", at[:, sl * 1024:(sl + 1) * 1024],
                    accs[p * 512 + tc * 128: p * 512 + (tc + 1) * 128, :], "SX",
                    waits=[("SX", sx_acw[15] if p == 0 and tc == 0 else 0),
                           ("SV", sv_f3.get((p, tc - 2, 1), 0)),
                           ("SO", so_out)])
                for fh in range(2):
                    for f2c in range(34):
                        w = []
                        if f2c == 0:
                            w = [("SA", sa_rs2), ("SV", sv_rs2), ("SG", sg_w2),
                                 ("SV", sv_f3.get((p, tc - 1, fh), 0)
                                  if tc >= 1 else sv_f3.get((p - 1, 3, fh), 0))]
                        if f2c < 32:
                            sp_f3[(p, tc, fh)] = sch.op(
                                "tensor",
                                lambda e, fh=fh, f2c=f2c, tc=tc: e.matmul(
                                    psF3[:, fh * 512:(fh + 1) * 512],
                                    h2[:, f2c * 512 + tc * 128:
                                       f2c * 512 + (tc + 1) * 128],
                                    W2sb[:, f2c * 1024 + fh * 512:
                                         f2c * 1024 + (fh + 1) * 512],
                                    start=(f2c == 0), stop=False),
                                inc=None, waits=w)
                        else:
                            lhs = rs2_hi if f2c == 32 else rs2_lo
                            sp_f3[(p, tc, fh)] = sch.op(
                                "tensor",
                                lambda e, fh=fh, tc=tc, lhs=lhs, f2c=f2c:
                                e.matmul(
                                    psF3[:, fh * 512:(fh + 1) * 512],
                                    lhs[0:1, tc * 128:(tc + 1) * 128],
                                    negh_bf[0:1, 0:512],
                                    start=False, stop=(f2c == 33)),
                                inc="SP" if f2c == 33 else None, waits=w)
                    sv_f3[(p, tc, fh)] = sch.op(
                        "vector",
                        lambda e, sl=sl, fh=fh: e.scalar_tensor_tensor(
                            ob[:, sl * 1024 + fh * 512:
                               sl * 1024 + (fh + 1) * 512],
                            psF3[:, fh * 512:(fh + 1) * 512], scl[:, 2:3],
                            at[:, sl * 1024 + fh * 512:
                               sl * 1024 + (fh + 1) * 512],
                            ALU.mult, ALU.add),
                        inc="SV",
                        waits=[("SP", sp_f3[(p, tc, fh)]), ("SX", sx_ac3)])
                sv_fen = sch.op(
                    "vector", lambda e: e.tensor_copy(fen_v[:], ob[:, 0:8]),
                    inc="SV", waits=[("SV", sv_f3[(p, tc, 1)])])
                so_out = sch.dma(
                    "sync",
                    out[p * 512 + tc * 128: p * 512 + (tc + 1) * 128, :],
                    ob[:, sl * 1024:(sl + 1) * 1024], "SO",
                    waits=[("SV", sv_fen)])
            prev_pass_sp = sch.cnt["SP"]

        sch.wait("sync", ("SO", sch.cnt["SO"]))
        sch.flush(block)
        pc.close()

    stack.close()
    return nc



def _build_abias(role):
    ab = np.full((H, 5, NDT, 128, 128), NEG, np.float32)
    ik = np.arange(128)[:, None]
    jq = np.arange(128)[None, :]
    for head in range(H):
        slope = SLOPES[head]
        for c in range(5):
            for dt in range(NDT):
                delta = jq - ik + 512 - 128 * dt
                valid = (delta >= 0) & (delta <= 512)
                if role == 0 and c < 4 and (c + dt) < 4:
                    valid = np.zeros_like(valid)
                if role == 1 or c == 4:
                    M = slope * 512.0 + C_STAB
                else:
                    M = slope * np.minimum(512.0, 128.0 * c + jq) + C_STAB
                ab[head, c, dt] = np.where(valid, slope * delta - M, NEG)
    return ab


def make_core_inputs(x, w_in, w_out, k0, k1, k2, s0, s1, s2):
    x = np.asarray(x)
    w_in = np.ascontiguousarray(np.asarray(w_in))
    w_out = np.ascontiguousarray(np.asarray(w_out))
    k0 = np.asarray(k0).astype(np.uint8)
    k1 = np.asarray(k1).astype(np.uint8)
    k2 = np.asarray(k2).astype(np.uint8)

    w_inT = np.ascontiguousarray(w_in.T)
    wg = np.empty((4, 8, 128, 768), np.float32)
    for g in range(4):
        for ec in range(8):
            rows = w_inT[ec * 128:(ec + 1) * 128]
            wg[g, ec, :, 0:256] = rows[:, 256 * g: 256 * g + 256]
            wg[g, ec, :, 256:512] = rows[:, 1024 + 256 * g: 1024 + 256 * g + 256]
            wg[g, ec, :, 512:768] = rows[:, 2048 + 256 * g: 2048 + 256 * g + 256]
    wo = np.ascontiguousarray(w_out.T.reshape(8, 128, 1024))
    k0r = np.ascontiguousarray(k0.reshape(16, 128, F1))
    k1r = np.ascontiguousarray(
        k1.reshape(32, 128, 32, 128).transpose(2, 1, 0, 3))
    k2r = np.ascontiguousarray(k2.reshape(32, 128, 1024))
    svec = np.ascontiguousarray(
        np.broadcast_to(
            np.array([np.float32(s0), np.float32(s1), np.float32(s2)],
                     np.float32)[None, :], (128, 3)))
    eye = np.eye(128, dtype=ml_dtypes.bfloat16)
    ab = [_build_abias(0), _build_abias(1)]

    ins = []
    for c in range(8):
        b, role = c // 2, c % 2
        xcr = np.empty((TOKH, E), np.float32)
        if role == 0:
            xcr[:HALO] = 0.0
            xcr[HALO:] = x[b, 0:TOK]
        else:
            xcr[:HALO] = x[b, TOK - HALO:TOK]
            xcr[HALO:] = x[b, TOK:]
        ins.append(dict(xc=np.ascontiguousarray(xcr), wg=wg, wo=wo, k0=k0r,
                        k1r=k1r, k2r=k2r, svec=svec, abias=ab[role], eye=eye))
    return ins


def kernel(L0_attn_x, L0_attn_w_in, L0_attn_w_out, L0_ffn_k0, L0_ffn_k1,
           L0_ffn_k2, L0_ffn_s0, L0_ffn_s1, L0_ffn_s2):
    ins = make_core_inputs(L0_attn_x, L0_attn_w_in, L0_attn_w_out, L0_ffn_k0,
                           L0_ffn_k1, L0_ffn_k2, L0_ffn_s0, L0_ffn_s1,
                           L0_ffn_s2)
    nc = build_program()
    res = run_bass_kernel_spmd(nc, ins, core_ids=list(range(8)))
    out = np.empty((B, S, E), np.float32)
    for c in range(8):
        b, role = c // 2, c % 2
        out[b, role * TOK:(role + 1) * TOK] = res.results[c]["out"]
    return out


# version 12
# speedup vs baseline: 1.0031x; 1.0031x over previous
import numpy as np
import ml_dtypes
from contextlib import ExitStack

import concourse.bass as bass
import concourse.mybir as mybir
from concourse.bass_utils import run_bass_kernel_spmd

F32 = mybir.dt.float32
BF16 = mybir.dt.bfloat16
U8 = mybir.dt.uint8
AF = mybir.ActivationFunctionType
ALU = mybir.AluOpType
AX = mybir.AxisListType

B, S, E = 4, 4096, 1024
H, D, WIN = 16, 64, 512
TOK = 2048
HALO = 512
TOKH = TOK + HALO
NCH = TOKH // 128
NQT = 16
NDT = 5
C_STAB = 10.0
SLOPES = [2.0 ** (-(h + 1) * 0.5) for h in range(H)]
F1 = 4096
NEG = -1.0e30


class Sch:
    ENGS = ("sync", "gpsimd", "tensor", "vector", "scalar")

    def __init__(self, nc, stack):
        self.nc = nc
        self.stack = stack
        self.ops = {e: [] for e in self.ENGS}
        self.sems = {}
        self.cnt = {}

    def sem(self, name):
        h = self.stack.enter_context(self.nc.semaphore(name))
        self.sems[name] = h
        self.cnt[name] = 0

    def op(self, eng, fn, inc=None, k=1, waits=()):
        w = {}
        for s, v in waits:
            if v and v > 0:
                w[s] = max(w.get(s, 0), v)
        n = None
        if inc is not None:
            self.cnt[inc] += k
            n = self.cnt[inc]
        sems = self.sems

        def thunk(e, fn=fn, w=tuple(w.items()), inc=inc, k=k):
            for s, v in w:
                e.wait_ge(sems[s], v)
            i = fn(e)
            if inc is not None:
                i.then_inc(sems[inc], k)

        self.ops[eng].append(thunk)
        return n

    def dma(self, eng, out, in_, inc, waits=()):
        return self.op(eng, lambda e: e.dma_start(out, in_), inc=inc, k=16,
                       waits=waits)

    def wait(self, eng, *waits):
        w = [(s, v) for (s, v) in waits if v and v > 0]
        if not w:
            return
        sems = self.sems

        def thunk(e, w=tuple(w)):
            for s, v in w:
                e.wait_ge(sems[s], v)

        self.ops[eng].append(thunk)

    def phase_barrier(self, engs=None):
        snap = [(s, self.cnt[s]) for s in self.sems]
        for eng in (engs or self.ENGS):
            self.wait(eng, *snap)

    def flush(self, block):
        for eng in self.ENGS:
            ops = self.ops[eng]
            if not ops:
                continue
            self.ops[eng] = []

            def body(e, ops=ops):
                for f in ops:
                    f(e)
            getattr(block, eng)(body)


def build_program(tap=None, stop_after=None, att_heads=4, qkv_parts="qkv",
                  detect_races=True, ffn2_serial=False, ffn2_nocorr=False,
                  ffn_stage_barriers=True):
    nc = bass.Bass("TRN2", target_bir_lowering=False,
                   detect_race_conditions=detect_races)
    stack = ExitStack()

    xc = nc.dram_tensor("xc", [TOKH, E], F32, kind="ExternalInput").ap()
    wg = nc.dram_tensor("wg", [4, 8, 128, 768], F32, kind="ExternalInput").ap()
    wo = nc.dram_tensor("wo", [8, 128, 1024], F32, kind="ExternalInput").ap()
    k0 = nc.dram_tensor("k0", [16, 128, F1], U8, kind="ExternalInput").ap()
    k1r = nc.dram_tensor("k1r", [32, 128, 32, 128], U8, kind="ExternalInput").ap()
    k2r = nc.dram_tensor("k2r", [32, 128, 1024], U8, kind="ExternalInput").ap()
    svec = nc.dram_tensor("svec", [128, 3], F32, kind="ExternalInput").ap()
    abias = nc.dram_tensor("abias", [H, 5, NDT, 128, 128], F32,
                           kind="ExternalInput").ap()
    eye = nc.dram_tensor("eye", [128, 128], BF16, kind="ExternalInput").ap()
    out = nc.dram_tensor("out", [TOK, E], F32, kind="ExternalOutput").ap()

    w0s = nc.dram_tensor("w0s", [32, 128, 1024], BF16, kind="Internal").ap()
    xns = nc.dram_tensor("xns", [8, 128, TOK], BF16, kind="Internal").ap()
    accs = nc.dram_tensor("accs", [TOK, E], F32, kind="Internal").ap()
    dend = nc.dram_tensor("dend", [H, NQT, 128], F32, kind="Internal").ap()

    tap_ap = None
    if tap == "xn":
        tap_ap = nc.dram_tensor("tap", [8, 128, TOKH], BF16, kind="ExternalOutput").ap()
    elif tap == "q":
        tap_ap = nc.dram_tensor("tap", [128, 2, TOK], BF16, kind="ExternalOutput").ap()
    elif tap == "k":
        tap_ap = nc.dram_tensor("tap", [128, 2, TOKH], BF16, kind="ExternalOutput").ap()
    elif tap == "v":
        tap_ap = nc.dram_tensor("tap", [128, NCH * 4 * 65], BF16, kind="ExternalOutput").ap()
    elif tap == "attn":
        tap_ap = nc.dram_tensor("tap", [128, 8, TOK], BF16, kind="ExternalOutput").ap()
    elif tap == "acc":
        tap_ap = nc.dram_tensor("tap", [TOK, E], F32, kind="ExternalOutput").ap()
    elif tap in ("h1", "h2"):
        tap_ap = nc.dram_tensor("tap", [128, 32, 512], BF16, kind="ExternalOutput").ap()

    sch = Sch(nc, stack)
    for s in ("SX", "SG", "SP", "SA", "SV", "SPL", "SO"):
        sch.sem(s)

    sb = lambda *a: stack.enter_context(nc.sbuf_tensor(*a))

    eye_sb = sb("eye_sb", [128, 128], BF16)
    svec_sb = sb("svec_sb", [128, 3], F32)
    scl = sb("scl", [128, 3], F32)
    ones_pe = sb("ones_pe", [128, 1], BF16)
    oneq = sb("oneq", [1, 128], F32)
    negh = sb("negh", [1, 512], F32)
    negh_bf = sb("negh_bf", [1, 512], BF16)
    rs1_sb = sb("rs1_sb", [1, 512], F32)
    rs2_sb = sb("rs2_sb", [1, 512], F32)
    rs1_hi = sb("rs1_hi", [1, 512], BF16)
    rs1_lo = sb("rs1_lo", [1, 512], BF16)
    rs2_hi = sb("rs2_hi", [1, 512], BF16)
    rs2_lo = sb("rs2_lo", [1, 512], BF16)
    at = sb("at", [128, 2 * 1024], F32)
    ob = sb("ob", [128, 2 * 1024], F32)
    fen_v = sb("fen_v", [128, 8], F32)
    fen_a = sb("fen_a", [128, 8], F32)
    fen_g = sb("fen_g", [128, 8], BF16)

    with nc.Block() as block:
        p0 = ExitStack()
        sb0 = lambda *a: p0.enter_context(nc.sbuf_tensor(*a))
        xn_stack = ExitStack()
        xn = xn_stack.enter_context(nc.sbuf_tensor("xn", [128, 8 * TOKH], BF16))

        def xn_v(ec, t0, n):
            return xn[:, ec * TOKH + t0: ec * TOKH + t0 + n]

        xtok = sb0("xtok", [128, 2 * 1024], F32)
        xnt = sb0("xnt", [128, 2 * 1024], BF16)
        sq = sb0("sq", [128, 1024], F32)
        xsum = sb0("xsum", [128, NCH], F32)
        xsqs = sb0("xsqs", [128, NCH], F32)
        tmc = sb0("tmc", [128, 2], F32)
        tve = sb0("tve", [128, 2], F32)
        tms = sb0("tms", [128, 2], F32)
        tsd = sb0("tsd", [128, 2], F32)
        trs = sb0("trs", [128, 2], F32)
        k0t = sb0("k0t", [128, 2 * F1], BF16)
        k0b = sb0("k0b", [128, 2 * F1], BF16)
        w0c = sb0("w0c", [128, 2 * F1], BF16)
        pst = p0.enter_context(nc.psum_tensor("pst", [128, 2048], BF16))

        sx_eye = sch.dma("sync", eye_sb[:], eye, "SX")
        sx_sv = sch.dma("sync", svec_sb[:], svec, "SX")
        sch.op("gpsimd", lambda e: e.memset(ones_pe[:], 1.0))
        sch.op("gpsimd", lambda e: e.memset(oneq[:], 1.0))
        sch.op("gpsimd", lambda e: e.memset(negh[:], -0.5))
        sch.op("gpsimd", lambda e: e.memset(negh_bf[:], -0.5))
        spl_ms = sch.op("gpsimd", lambda e: e.memset(xsqs[:], 0.0), inc="SPL")
        spl_init = spl_ms
        sv_scl = None
        for j, f in ((0, 2.0), (1, 4.0), (2, 4.0)):
            sv_scl = sch.op(
                "vector",
                lambda e, j=j, f=f: e.tensor_scalar_mul(
                    scl[:, j:j + 1], svec_sb[:, j:j + 1], f),
                inc="SV", waits=[("SX", sx_sv)])

        sv_stt = [0] * NCH
        sa_sq = [0] * NCH
        sp_tr = [0] * NCH
        sa_ev = [0] * NCH
        c1 = 1.0 / 1024.0

        spl_sub = [0] * 8
        sx_w0 = [0] * 8
        sg_k0 = [0] * 8

        def emit_k0(ec):
            sl = ec % 2
            sch.dma("gpsimd", k0t[:, sl * F1:(sl + 1) * F1], k0[ec], "SG",
                    waits=[("SPL", spl_sub[ec - 2] if ec >= 2 else 0)])
            sg_k0[ec] = sch.dma("gpsimd", k0b[:, sl * F1:(sl + 1) * F1],
                                k0[ec + 8], "SG")
            spl_sub[ec] = sch.op(
                "gpsimd",
                lambda e, sl=sl: e.tensor_sub(w0c[:, sl * F1:(sl + 1) * F1],
                                              k0t[:, sl * F1:(sl + 1) * F1],
                                              k0b[:, sl * F1:(sl + 1) * F1]),
                inc="SPL",
                waits=[("SG", sg_k0[ec]), ("SX", sx_w0[ec - 2] if ec >= 2 else 0)])
            spl_sub[ec] = sch.op(
                "gpsimd",
                lambda e, sl=sl: e.tensor_copy(fen_g[:],
                                               w0c[:, sl * F1: sl * F1 + 8]),
                inc="SPL", waits=[("SPL", spl_sub[ec])])
            for f1c in range(32):
                sx_w0[ec] = sch.dma(
                    "sync", w0s[f1c][:, ec * 128:(ec + 1) * 128],
                    w0c[:, sl * F1 + f1c * 128: sl * F1 + (f1c + 1) * 128], "SX",
                    waits=([("SPL", spl_sub[ec])] if f1c == 0 else []))

        for tc in range(NCH):
            sl = tc % 2
            sx_x = sch.dma(
                "sync", xtok[:, sl * 1024:(sl + 1) * 1024],
                xc[tc * 128:(tc + 1) * 128, :], "SX",
                waits=[("SV", sv_stt[tc - 2] if tc >= 2 else 0),
                       ("SA", sa_sq[tc - 2] if tc >= 2 else 0)])
            xt = xtok[:, sl * 1024:(sl + 1) * 1024]
            sch.op("vector",
                   lambda e, xt=xt, tc=tc: e.reduce_sum(
                       xsum[:, tc:tc + 1], xt, axis=AX.X),
                   inc="SV", waits=[("SX", sx_x)])
            sa_sq[tc] = sch.op(
                "scalar",
                lambda e, xt=xt, tc=tc: e.activation(
                    sq[:], xt, AF.Square, accum_out=xsqs[:, tc:tc + 1]),
                inc="SA", waits=[("SX", sx_x), ("SPL", spl_ms)])
            sa_sq[tc] = sch.op(
                "scalar",
                lambda e, tc=tc: e.copy(fen_a[0:1, 0:1], xsqs[0:1, tc:tc + 1]),
                inc="SA", waits=[("SA", sa_sq[tc])])
            sch.op("vector",
                   lambda e, tc=tc, sl=sl: e.tensor_scalar(
                       tve[:, sl:sl + 1], xsqs[:, tc:tc + 1], c1, 1e-5,
                       ALU.mult, ALU.add),
                   inc="SV", waits=[("SA", sa_sq[tc])])
            sv_m = sch.op("vector",
                          lambda e, tc=tc, sl=sl: e.tensor_scalar_mul(
                              tmc[:, sl:sl + 1], xsum[:, tc:tc + 1], c1),
                          inc="SV")
            sv_ms = sch.op("vector",
                           lambda e, sl=sl: e.tensor_mul(
                               tms[:, sl:sl + 1], tmc[:, sl:sl + 1],
                               tmc[:, sl:sl + 1]),
                           inc="SV", waits=[("SV", sv_m)])
            sv_ve = sch.op("vector",
                           lambda e, sl=sl: e.tensor_sub(
                               tve[:, sl:sl + 1], tve[:, sl:sl + 1],
                               tms[:, sl:sl + 1]), inc="SV",
                           waits=[("SV", sv_ms)])
            sa_sd = sch.op("scalar",
                           lambda e, sl=sl: e.sqrt(tsd[:, sl:sl + 1],
                                                   tve[:, sl:sl + 1]),
                           inc="SA", waits=[("SV", sv_ve)])
            sa_sd = sch.op("scalar",
                           lambda e, sl=sl: e.copy(fen_a[0:1, 0:1],
                                                   tsd[0:1, sl:sl + 1]),
                           inc="SA", waits=[("SA", sa_sd)])
            sv_rc = sch.op("vector",
                           lambda e, sl=sl: e.reciprocal(trs[:, sl:sl + 1],
                                                         tsd[:, sl:sl + 1]),
                           inc="SV", waits=[("SA", sa_sd)])
            xnv = xnt[:, sl * 1024:(sl + 1) * 1024]
            sv_stt[tc] = sch.op(
                "vector",
                lambda e, xt=xt, xnv=xnv, sl=sl: e.tensor_scalar(
                    xnv, xt, tmc[:, sl:sl + 1], trs[:, sl:sl + 1],
                    ALU.subtract, ALU.mult),
                inc="SV", waits=[("SP", sp_tr[tc - 2] if tc >= 2 else 0),
                                 ("SV", sv_rc)])
            sv_stt[tc] = sch.op(
                "vector",
                lambda e, xnv=xnv: e.tensor_copy(fen_v[:], xnv[:, 0:8]),
                inc="SV", waits=[("SV", sv_stt[tc])])
            pslot = pst[:, (tc % 2) * 1024:(tc % 2) * 1024 + 1024]
            for j in range(8):
                sp_tr[tc] = sch.op(
                    "tensor",
                    lambda e, xnv=xnv, j=j, pslot=pslot: e.transpose(
                        pslot[:, j * 128:(j + 1) * 128],
                        xnv[:, j * 128:(j + 1) * 128], eye_sb[:]),
                    inc="SP",
                    waits=([("SV", sv_stt[tc]), ("SX", sx_eye),
                            ("SA", sa_ev[tc - 2] if tc >= 2 else 0)]
                           if j == 0 else []))
            xn_dst = xn[:].rearrange("p (c t) -> p c t", c=8)[
                :, :, tc * 128:(tc + 1) * 128]
            pst_v = pslot.rearrange("p (c t) -> p c t", c=8)
            sa_ev[tc] = sch.op(
                "scalar",
                lambda e, xn_dst=xn_dst, pst_v=pst_v: e.copy(xn_dst, pst_v),
                inc="SA", waits=[("SP", sp_tr[tc])])
            if tc >= 4 and tc % 2 == 0 and tc // 2 - 2 < 8:
                emit_k0(tc // 2 - 2)

        sa_fence = sch.op("scalar",
                          lambda e: e.copy(fen_a[:], xn[:, 0:8]),
                          inc="SA", waits=[("SA", sa_ev[NCH - 1])])
        sx_xns = 0
        for ec in range(8):
            sx_xns = sch.dma("sync", xns[ec], xn_v(ec, HALO, TOK), "SX",
                             waits=[("SA", sa_fence)])
        if tap == "xn":
            for ec in range(8):
                sch.dma("sync", tap_ap[ec], xn_v(ec, 0, TOKH), "SO",
                        waits=[("SA", sa_ev[NCH - 1])])

        if stop_after == "p0":
            sch.wait("sync", ("SO", sch.cnt["SO"]), ("SX", sch.cnt["SX"]))
            sch.flush(block)
            p0.close()
            xn_stack.close()
            stack.close()
            return nc
        sch.phase_barrier()
        sch.flush(block)
        p0.close()

        pb = ExitStack()
        sbB = lambda *a: pb.enter_context(nc.sbuf_tensor(*a))
        psB_ = lambda *a: pb.enter_context(nc.psum_tensor(*a))
        q_sb = sbB("q_sb", [128, 2, TOK], BF16)
        k_sb = sbB("k_sb", [128, 2, TOKH], BF16)
        v_sb = sbB("v_sb", [128, NCH, 4, 65], BF16)
        o_sb = sbB("o_sb", [128, 8, TOK], BF16)
        wgb = sbB("wgb", [128, 8, 768], BF16)
        ab_sb = sbB("ab_sb", [128, 5, NDT * 128], F32)
        p_sb = sbB("p_sb", [128, 3, NDT * 128], BF16)
        den_s = sbB("den_s", [1, 2 * 128], F32)
        den_all = sbB("den_all", [128, 256], F32)
        db_row = sbB("db_row", [1, 2 * 2048], F32)
        Dsb = sbB("Dsb", [128, 2 * 1024], F32)
        psS3 = psB_("psS3", [128, 3072], F32)
        psPV = psB_("psPV", [128, 1024], F32)

        sg_wg = [0] * 4
        sp_qkv = [0] * 4
        sp_att = [0] * 4
        sv_att = [0] * 4
        sa_att = [0] * 4
        it = 0
        sa_exp = {}
        sp_pv = {}
        sp_sc = {}
        sa_oev = {}
        sx_den = {}
        sp_nb = {}
        sa_nb = {}
        sv_nb = {}
        sv_ab = {}
        spl_ab = {}
        sv_vev = [0] * 4
        sa_qk = [0] * 4

        for g in range(4):
            for ec in range(8):
                n = sch.dma(
                    "gpsimd", wgb[:, ec, :], wg[g, ec], "SG",
                    waits=([("SP", sp_att[g - 1] if g >= 1 else 0)]
                           if ec == 0 else []))
            sg_wg[g] = n
            spl_v1 = sch.op(
                "gpsimd", lambda e: e.memset(v_sb[:, :, :, 64:65], 1.0),
                inc="SPL", waits=[("SP", sp_att[g - 1] if g >= 1 else 0)])
            qkv_entry = [("SG", sg_wg[g])]
            pp = 0
            sa_last_evac = {0: 0, 1: 0}
            for mc in range(2):
                for what, nspan, cols in (("q", 4, 0), ("k", 5, 256)):
                    if what not in qkv_parts:
                        continue
                    for span in range(nspan):
                        pv = psS3[:, pp * 1024: pp * 1024 + 512]
                        t0 = (HALO if what == "q" else 0) + span * 512
                        ready = sa_ev[min(NCH - 1, t0 // 128 + 3)]
                        for ec in range(8):
                            w = (list(qkv_entry) + [("SA", sa_last_evac[pp]),
                                                    ("SA", ready)]
                                 if ec == 0 else [])
                            n = sch.op(
                                "tensor",
                                lambda e, pv=pv, ec=ec, mc=mc, cols=cols, t0=t0:
                                e.matmul(
                                    pv,
                                    wgb[:, ec, cols + mc * 128: cols + (mc + 1) * 128],
                                    xn_v(ec, t0, 512),
                                    start=(ec == 0), stop=(ec == 7)),
                                inc="SP" if ec == 7 else None, waits=w)
                        sp_qkv[g] = n
                        if what == "q":
                            dst = q_sb[:, mc, t0 - HALO: t0 - HALO + 512]
                            scale = 0.125
                        else:
                            dst = k_sb[:, mc, t0: t0 + 512]
                            scale = 1.0
                        sa_qk[g] = sa_last_evac[pp] = sch.op(
                            "scalar",
                            lambda e, pv=pv, dst=dst, scale=scale:
                            e.activation(dst, pv, AF.Copy, scale=scale),
                            inc="SA", waits=[("SP", sp_qkv[g])])
                        pp ^= 1
            sa_qk[g] = sch.op(
                "scalar", lambda e: e.copy(fen_a[:], k_sb[:, 0, 0:8]),
                inc="SA", waits=[("SA", sa_qk[g])])
            vpp = 0
            sv_last_vev = {0: 0, 1: 0}
            nv_ch = NCH if "v" in qkv_parts else (2 if "w" in qkv_parts else 0)
            for tc in range(nv_ch):
                pv = psS3[:, 2048 + vpp * 512: 2048 + vpp * 512 + 256]
                for ec in range(8):
                    w = ([("SPL", spl_v1), ("SV", sv_last_vev[vpp]),
                          ("SA", sa_ev[tc])]
                         + (qkv_entry if tc == 0 else []) if ec == 0 else [])
                    n = sch.op(
                        "tensor",
                        lambda e, pv=pv, ec=ec, tc=tc: e.matmul(
                            pv, xn_v(ec, tc * 128, 128), wgb[:, ec, 512:768],
                            start=(ec == 0), stop=(ec == 7)),
                        inc="SP" if ec == 7 else None, waits=w)
                sp_qkv[g] = n
                pvv = pv.rearrange("p (h d) -> p h d", h=4)
                sv_vev[g] = sv_last_vev[vpp] = sch.op(
                    "vector",
                    lambda e, pvv=pvv, tc=tc: e.tensor_copy(
                        v_sb[:, tc, :, 0:64], pvv),
                    inc="SV", waits=[("SP", sp_qkv[g])])
                vpp ^= 1
            sv_vev[g] = sch.op(
                "vector",
                lambda e: e.tensor_copy(fen_v[:], v_sb[:, 0, 0, 0:8]),
                inc="SV", waits=[("SV", sv_vev[g])])

            def emit_pv(ctx):
                cit, chh, cpo, cjg, chead, cqt, cslot = ctx
                vslot = cit % 2
                pvs = psPV[:, vslot * 512: vslot * 512 + 128]
                for dt in range(NDT):
                    kc = cqt + dt
                    w = [("SA", sa_exp[cit])]
                    if dt == 0:
                        w.append(("SA", sa_oev.get(cit - 2, 0)))
                    sp_pv[cit] = sch.op(
                        "tensor",
                        lambda e, pvs=pvs, kc=kc, chh=chh, cslot=cslot, dt=dt:
                        e.matmul(
                            pvs[0:65, :], v_sb[:, kc, chh, :],
                            p_sb[:, cslot, dt * 128:(dt + 1) * 128],
                            start=(dt == 0), stop=(dt == NDT - 1)),
                        inc="SP" if dt == NDT - 1 else None, waits=w)
                sch.op(
                    "scalar",
                    lambda e, pvs=pvs, cpo=cpo, cjg=cjg, cqt=cqt: e.copy(
                        o_sb[cpo:cpo + 64, cjg, cqt * 128:(cqt + 1) * 128],
                        pvs[0:64, :]),
                    inc="SA", waits=[("SP", sp_pv[cit])])
                sa_oev[cit] = sch.op(
                    "scalar",
                    lambda e, pvs=pvs, vslot=vslot: e.copy(
                        den_s[0:1, vslot * 128:(vslot + 1) * 128],
                        pvs[64:65, :]),
                    inc="SA",
                    waits=[("SX", sx_den.get(cit - 2, 0))])
                sa_oev[cit] = sch.op(
                    "scalar", lambda e: e.copy(fen_a[0:1, :], den_s[0:1, 0:8]),
                    inc="SA", waits=[("SA", sa_oev[cit])])
                sx_den[cit] = sch.dma(
                    "sync", dend[chead, cqt],
                    den_s[0:1, vslot * 128:(vslot + 1) * 128], "SX",
                    waits=[("SA", sa_oev[cit])])

            sv_adds = {}

            def emit_exp(ctx):
                cit, chh, cpo, cjg, chead, cqt, cslot = ctx
                psSl = psS3[:, cslot * 1024: cslot * 1024 + 640]
                sa_exp[cit] = sch.op(
                    "scalar",
                    lambda e, psSl=psSl, cslot=cslot: e.activation(
                        p_sb[:, cslot, :], psSl, AF.Exp),
                    inc="SA",
                    waits=[("SV", sv_adds[cit]), ("SP", sp_pv.get(cit - 3, 0))])

            q_exp = []
            q_pv = []
            for hh in range(att_heads):
                head = 4 * g + hh
                po = 64 * (hh % 2)
                mc = hh // 2
                jg = 2 * g + mc
                sx_ab = 0
                for c in range(5):
                    sx_ab = sch.dma(
                        "sync",
                        ab_sb[:, c].rearrange("p (d q) -> p d q", d=NDT),
                        abias[head, c].rearrange("d p q -> p d q"), "SX",
                        waits=([("SV", sv_ab.get(head - 1, 0))]
                               if c == 0 else []))
                for qt in range(NQT):
                    cls = min(qt, 4)
                    slot = it % 3
                    psS = psS3[:, slot * 1024: slot * 1024 + 640]
                    for dt in range(NDT):
                        kc = qt + dt
                        w = []
                        if dt == 0:
                            w = [("SA", sa_exp.get(it - 3, 0)),
                                 ("SA", sa_qk[g]), ("SV", sv_vev[g])]
                        n = sch.op(
                            "tensor",
                            lambda e, psS=psS, dt=dt, po=po, mc=mc, kc=kc, qt=qt:
                            e.matmul(
                                psS[:, dt * 128:(dt + 1) * 128],
                                k_sb[po:po + 64, mc, kc * 128:(kc + 1) * 128],
                                q_sb[po:po + 64, mc, qt * 128:(qt + 1) * 128],
                                start=True, stop=True),
                            inc="SP" if dt == NDT - 1 else None, waits=w)
                    sp_sc[it] = n
                    sv_adds[it] = sch.op(
                        "vector",
                        lambda e, psS=psS, cls=cls: e.tensor_add(
                            psS, psS, ab_sb[:, cls]),
                        inc="SV",
                        waits=[("SP", sp_sc[it]), ("SX", sx_ab)])
                    sv_ab[head] = sv_adds[it]
                    ctx = (it, hh, po, jg, head, qt, slot)
                    q_exp.append(ctx)
                    if len(q_exp) > 1:
                        emit_exp(q_exp.pop(0))
                    q_pv.append(ctx)
                    if len(q_pv) > 2:
                        emit_pv(q_pv.pop(0))
                    it += 1
            for ctx in q_exp:
                emit_exp(ctx)
            while q_pv:
                emit_pv(q_pv.pop(0))
            sp_att[g] = sch.cnt["SP"]
            sv_att[g] = sch.cnt["SV"]
            sa_att[g] = sch.cnt["SA"]

        sch.op("scalar", lambda e: e.copy(fen_a[:], o_sb[:, 0, 0:8]),
               inc="SA", waits=[("SA", sch.cnt["SA"])])
        att_sa = sch.cnt["SA"]
        att_sx = sch.cnt["SX"]
        sx_da = sch.dma("sync", den_all[:],
                        dend[:].rearrange("h (a b) d -> (h a) (b d)", b=2), "SX",
                        waits=[("SX", att_sx)])
        sv_da = sch.op("vector", lambda e: e.reciprocal(den_all[:], den_all[:]),
                       inc="SV", waits=[("SX", sx_da)])
        sv_da = sch.op("vector", lambda e: e.tensor_copy(fen_v[:], den_all[:, 0:8]),
                       inc="SV", waits=[("SV", sv_da)])
        sx_dw = sch.dma("sync",
                        dend[:].rearrange("h (a b) d -> (h a) (b d)", b=2),
                        den_all[:], "SX", waits=[("SV", sv_da)])
        sx_dbr = {}
        for head in range(H if att_heads == 4 else 0):
            po = 64 * (head % 2)
            jg = head // 2
            hs = head % 2
            sx_dbr[head] = sch.dma(
                "sync", db_row[0:1, hs * 2048:(hs + 1) * 2048],
                dend[head].rearrange("q d -> (q d)"), "SX",
                waits=[("SX", sx_dw), ("SP", sp_nb.get(2 * (head - 2) + 1, 0))])
            for j in range(2):
                gq = head * 2 + j
                slot = gq % 2
                psN = psS3[:, slot * 1024:(slot + 1) * 1024]
                w = [("SX", sx_dbr[head]), ("SA", sa_nb.get(gq - 2, 0)),
                     ("SPL", spl_init if gq == 0 else 0), ("SA", att_sa)]
                for half in range(2):
                    sp_nb[gq] = sch.op(
                        "tensor",
                        lambda e, psN=psN, hs=hs, j=j, half=half: e.matmul(
                            psN[:, half * 512:(half + 1) * 512], oneq[:],
                            db_row[0:1, hs * 2048 + j * 1024 + half * 512:
                                   hs * 2048 + j * 1024 + (half + 1) * 512],
                            start=True, stop=True),
                        inc="SP" if half == 1 else None,
                        waits=(w if half == 0 else []))
                sa_nb[gq] = sch.op(
                    "scalar",
                    lambda e, psN=psN, slot=slot: e.copy(
                        Dsb[:, slot * 1024:(slot + 1) * 1024], psN),
                    inc="SA",
                    waits=[("SP", sp_nb[gq]), ("SV", sv_nb.get(gq - 2, 0))])
                sv_nb[gq] = sch.op(
                    "vector",
                    lambda e, po=po, jg=jg, j=j, slot=slot: e.tensor_mul(
                        o_sb[po:po + 64, jg, j * 1024:(j + 1) * 1024],
                        o_sb[po:po + 64, jg, j * 1024:(j + 1) * 1024],
                        Dsb[po:po + 64, slot * 1024:(slot + 1) * 1024]),
                    inc="SV", waits=[("SA", sa_nb[gq])])
        sch.op("vector", lambda e: e.tensor_copy(fen_v[:], o_sb[:, 0, 0:8]),
               inc="SV", waits=[("SV", sch.cnt["SV"])])
        sv_norm = sch.cnt["SV"]
        norm_sa = sch.cnt["SA"]

        if tap == "q":
            sch.wait("sync", ("SA", sch.cnt["SA"]))
            for mcx in range(2):
                sch.dma("sync", tap_ap[:, mcx, :], q_sb[:, mcx, :], "SO")
        if tap == "k":
            sch.wait("sync", ("SA", sch.cnt["SA"]))
            for mcx in range(2):
                sch.dma("sync", tap_ap[:, mcx, :], k_sb[:, mcx, :], "SO")
        if tap == "v":
            sch.wait("sync", ("SV", sch.cnt["SV"]))
            sch.dma("sync", tap_ap[:],
                    v_sb[:].rearrange("p c h d -> p (c h d)"), "SO")
        if stop_after == "qkv":
            sch.wait("sync", ("SO", sch.cnt["SO"]), ("SX", sch.cnt["SX"]),
                     ("SA", sch.cnt["SA"]), ("SV", sch.cnt["SV"]))
            sch.flush(block)
            pb.close()
            xn_stack.close()
            stack.close()
            return nc
        if tap == "attn":
            sch.wait("sync", ("SV", sv_norm))
            for jgx in range(8):
                sch.dma("sync", tap_ap[:, jgx, :], o_sb[:, jgx, :], "SO")

        wob = sbB("wob", [128, 8, 1024], BF16)
        sg_wo = 0
        for ec in range(8):
            sg_wo = sch.dma("gpsimd", wob[:, ec, :], wo[ec], "SG")
        sp_opA = {}
        sv_opadd = {}
        sx_acw = {}
        for tc in range(16):
            sl = tc % 2
            sx_opx = sch.dma(
                "sync", at[:, sl * 1024:(sl + 1) * 1024],
                xc[HALO + tc * 128: HALO + (tc + 1) * 128, :], "SX",
                waits=[("SV", sv_opadd.get(tc - 2, 0)),
                       ("SX", sx_acw.get(tc - 2, 0))])
            for half in range(2):
                pv = psS3[:, half * 512:(half + 1) * 512]
                for ec in range(8):
                    w = []
                    if ec == 0:
                        w = [("SV", sv_norm), ("SG", sg_wo),
                             ("SV", sv_opadd.get(tc - 1, 0)),
                             ("SA", norm_sa)]
                    sp_opA[(tc, half)] = sch.op(
                        "tensor",
                        lambda e, pv=pv, ec=ec, tc=tc, half=half: e.matmul(
                            pv, o_sb[:, ec, tc * 128:(tc + 1) * 128],
                            wob[:, ec, half * 512:(half + 1) * 512],
                            start=(ec == 0), stop=(ec == 7)),
                        inc="SP" if ec == 7 else None, waits=w)
                sv_opadd[tc] = sch.op(
                    "vector",
                    lambda e, pv=pv, sl=sl, half=half: e.tensor_add(
                        at[:, sl * 1024 + half * 512:
                           sl * 1024 + (half + 1) * 512],
                        pv,
                        at[:, sl * 1024 + half * 512:
                           sl * 1024 + (half + 1) * 512]),
                    inc="SV", waits=[("SP", sp_opA[(tc, half)]), ("SX", sx_opx)])
            sv_opadd[tc] = sch.op(
                "vector", lambda e: e.tensor_copy(fen_v[:], at[:, 0:8]),
                inc="SV", waits=[("SV", sv_opadd[tc])])
            sx_acw[tc] = sch.dma(
                "sync", accs[tc * 128:(tc + 1) * 128, :],
                at[:, sl * 1024:(sl + 1) * 1024], "SX",
                waits=[("SV", sv_opadd[tc])])
        if tap == "acc":
            sch.dma("sync", tap_ap[:], accs[:], "SO",
                    waits=[("SX", sx_acw[15])])

        if stop_after == "att":
            sch.wait("sync", ("SO", sch.cnt["SO"]), ("SX", sch.cnt["SX"]))
            sch.flush(block)
            pb.close()
            xn_stack.close()
            stack.close()
            return nc
        sch.phase_barrier()
        sch.flush(block)
        pb.close()
        xn_stack.close()

        pc = ExitStack()
        sbC = lambda *a: pc.enter_context(nc.sbuf_tensor(*a))
        psC = lambda *a: pc.enter_context(nc.psum_tensor(*a))
        h2 = sbC("h2", [128, 32 * 512], BF16)
        W2sb = sbC("W2sb", [128, 32 * 1024], BF16)
        h1 = sbC("h1", [128, 32 * 512], BF16)
        k1b = sbC("k1b", [128, 2 * 4096], BF16)
        w0b = sbC("w0b", [128, 2 * 1024], BF16)
        xnp = sbC("xnp", [128, 8 * 512], BF16)
        den_flush = sbC("den_flush", [1, 128], BF16)
        psF = psC("psF", [128, 1024], F32)
        psF3 = psC("psF3", [128, 1024], F32)
        psRS = psC("psRS", [128, 512], F32)

        sg_w2 = 0
        for f2c in range(32):
            sg_w2 = sch.dma("gpsimd", W2sb[:, f2c * 1024:(f2c + 1) * 1024],
                            k2r[f2c], "SG")

        sp_f1 = {}
        sa_r1 = {}
        sp_f2 = {}
        sa_r2 = {}
        sp_f3 = {}
        sv_f3 = {}
        sa_rs1 = sa_rs2 = 0
        so_out = 0
        prev_pass_sp = 0
        sx_xnp = 0

        for p in range(4):
            w = [("SX", sx_xns), ("SP", prev_pass_sp)]
            for ec in range(8):
                sx_xnp = sch.dma("sync", xnp[:, ec * 512:(ec + 1) * 512],
                                 xns[ec][:, p * 512:(p + 1) * 512], "SX",
                                 waits=(w if ec == 0 else []))
            for f1c in range(32):
                sl = f1c % 2
                sx_w0b = sch.dma(
                    "sync", w0b[:, sl * 1024:(sl + 1) * 1024], w0s[f1c], "SX",
                    waits=[("SX", sx_w0[7] if p == 0 and f1c < 2 else 0),
                           ("SP", sp_f1.get((p, f1c - 2), prev_pass_sp))])
                for ec in range(8):
                    w = []
                    if ec == 0:
                        w = [("SX", sx_w0b), ("SX", sx_xnp),
                             ("SA", sa_r1.get((p, f1c - 2), 0)),
                             ("SA", sa_rs2 if p > 0 else 0),
                             ("SV", sv_f3.get((p - 1, 3, 1), 0))]
                    sp_f1[(p, f1c)] = sch.op(
                        "tensor",
                        lambda e, sl=sl, ec=ec: e.matmul(
                            psF[:, sl * 512:(sl + 1) * 512],
                            w0b[:, sl * 1024 + ec * 128:
                                sl * 1024 + (ec + 1) * 128],
                            xnp[:, ec * 512:(ec + 1) * 512],
                            start=(ec == 0), stop=(ec == 7)),
                        inc="SP" if ec == 7 else None, waits=w)
                sa_r1[(p, f1c)] = sch.op(
                    "scalar",
                    lambda e, sl=sl, f1c=f1c: e.activation(
                        h1[:, f1c * 512:(f1c + 1) * 512],
                        psF[:, sl * 512:(sl + 1) * 512], AF.Relu,
                        scale=scl[:, 0:1]),
                    inc="SA", waits=[("SP", sp_f1[(p, f1c)]), ("SV", sv_scl)])
                for rc in ([f1c - 1] if f1c >= 1 else []):
                    w = [("SA", sa_r1.get((p, rc + 1), sa_r1[(p, rc)]))]
                    if rc == 0:
                        w += [("SA", sa_rs1), ("SPL", spl_init)]
                    sch.op(
                        "tensor",
                        lambda e, rc=rc: e.matmul(
                            psRS[0:1, 0:512], ones_pe[:],
                            h1[:, rc * 512:(rc + 1) * 512],
                            start=(rc == 0), stop=False),
                        inc=None, waits=w)
            if ffn_stage_barriers:
                sch.phase_barrier()
            fa = sch.op("scalar",
                        lambda e: e.copy(fen_a[:],
                                         h1[:, 31 * 512:31 * 512 + 8]),
                        inc="SA", waits=[("SA", sa_r1[(p, 31)])])
            sp_rs1 = sch.op(
                "tensor",
                lambda e: e.matmul(
                    psRS[0:1, 0:512], ones_pe[:],
                    h1[:, 31 * 512:32 * 512],
                    start=False, stop=True),
                inc="SP", waits=[("SA", fa)])
            sa_rs1 = sch.op("scalar",
                            lambda e: e.copy(rs1_sb[:], psRS[0:1, 0:512]),
                            inc="SA", waits=[("SP", sp_rs1)])
            sa_rs1 = sch.op("scalar",
                            lambda e: e.copy(rs1_hi[:], rs1_sb[:]),
                            inc="SA", waits=[("SA", sa_rs1)])
            sa_rs1 = sch.op("scalar",
                            lambda e: e.copy(fen_a[0:1, :], rs1_hi[0:1, 0:8]),
                            inc="SA", waits=[("SA", sa_rs1)])
            sv_rs1 = sch.op("vector",
                            lambda e: e.tensor_sub(rs1_lo[:], rs1_sb[:],
                                                   rs1_hi[:]),
                            inc="SV", waits=[("SA", sa_rs1)])
            sv_rs1 = sch.op("vector",
                            lambda e: e.tensor_copy(fen_v[0:1, :],
                                                    rs1_lo[0:1, 0:8]),
                            inc="SV", waits=[("SV", sv_rs1)])
            for f2c in range(32):
                sl = f2c % 2
                sg_k1 = sch.dma(
                    "gpsimd", k1b[:, sl * 4096:(sl + 1) * 4096],
                    k1r[f2c].rearrange("p c t -> p (c t)"), "SG",
                    waits=[("SP", sp_f2.get((p, f2c - 2),
                                            sp_f2.get((p - 1, 30 + f2c), 0)))])
                for f1c in range(34):
                    w = []
                    if f1c == 0:
                        w = [("SG", sg_k1), ("SA", sa_rs1), ("SV", sv_rs1),
                             ("SA", sa_r2.get((p, f2c - 2), 0))]
                        if ffn2_serial:
                            w.append(("SA", sa_r2.get((p, f2c - 1), 0)))
                            w.append(("SG", sch.cnt["SG"]))
                            w.append(("SX", sch.cnt["SX"]))
                    if f1c < 32:
                        sp_f2[(p, f2c)] = sch.op(
                            "tensor",
                            lambda e, sl=sl, f1c=f1c: e.matmul(
                                psF[:, sl * 512:(sl + 1) * 512],
                                k1b[:, sl * 4096 + f1c * 128:
                                    sl * 4096 + (f1c + 1) * 128],
                                h1[:, f1c * 512:(f1c + 1) * 512],
                                start=(f1c == 0),
                                stop=(ffn2_nocorr and f1c == 31)),
                            inc="SP" if (ffn2_nocorr and f1c == 31) else None,
                            waits=w)
                    elif not ffn2_nocorr:
                        rhs = rs1_hi if f1c == 32 else rs1_lo
                        sp_f2[(p, f2c)] = sch.op(
                            "tensor",
                            lambda e, sl=sl, rhs=rhs, f1c=f1c: e.matmul(
                                psF[:, sl * 512:(sl + 1) * 512],
                                negh_bf[0:1, 0:128], rhs[:],
                                start=False, stop=(f1c == 33)),
                            inc="SP" if f1c == 33 else None, waits=w)
                sa_r2[(p, f2c)] = sch.op(
                    "scalar",
                    lambda e, sl=sl, f2c=f2c: e.activation(
                        h2[:, f2c * 512:(f2c + 1) * 512],
                        psF[:, sl * 512:(sl + 1) * 512], AF.Relu,
                        scale=scl[:, 1:2]),
                    inc="SA", waits=[("SP", sp_f2[(p, f2c)])])
                for rc in ([f2c - 1] if f2c >= 1 else []):
                    w = [("SA", sa_r2.get((p, rc + 1), sa_r2[(p, rc)]))]
                    if rc == 0:
                        w.append(("SA", sa_rs2))
                    sch.op(
                        "tensor",
                        lambda e, rc=rc: e.matmul(
                            psRS[0:1, 0:512], ones_pe[:],
                            h2[:, rc * 512:(rc + 1) * 512],
                            start=(rc == 0), stop=False),
                        inc=None, waits=w)
            if ffn_stage_barriers:
                sch.phase_barrier()
            fa = sch.op("scalar",
                        lambda e: e.copy(fen_a[:],
                                         h2[:, 31 * 512:31 * 512 + 8]),
                        inc="SA", waits=[("SA", sa_r2[(p, 31)])])
            sp_rs2 = sch.op(
                "tensor",
                lambda e: e.matmul(
                    psRS[0:1, 0:512], ones_pe[:],
                    h2[:, 31 * 512:32 * 512],
                    start=False, stop=True),
                inc="SP", waits=[("SA", fa)])
            sa_rs2 = sch.op("scalar",
                            lambda e: e.copy(rs2_sb[:], psRS[0:1, 0:512]),
                            inc="SA", waits=[("SP", sp_rs2)])
            sa_rs2 = sch.op("scalar",
                            lambda e: e.copy(rs2_hi[:], rs2_sb[:]),
                            inc="SA", waits=[("SA", sa_rs2)])
            sa_rs2 = sch.op("scalar",
                            lambda e: e.copy(fen_a[0:1, :], rs2_hi[0:1, 0:8]),
                            inc="SA", waits=[("SA", sa_rs2)])
            sv_rs2 = sch.op("vector",
                            lambda e: e.tensor_sub(rs2_lo[:], rs2_sb[:],
                                                   rs2_hi[:]),
                            inc="SV", waits=[("SA", sa_rs2)])
            sv_rs2 = sch.op("vector",
                            lambda e: e.tensor_copy(fen_v[0:1, :],
                                                    rs2_lo[0:1, 0:8]),
                            inc="SV", waits=[("SV", sv_rs2)])
            if tap == "h1" and p == 0:
                sch.wait("sync", ("SA", sa_r1[(0, 31)]))
                sch.dma("sync", tap_ap[:].rearrange("p c t -> p (c t)"),
                        h1[:], "SO")
            if tap == "h2" and p == 0:
                sch.wait("sync", ("SA", sa_fl))
                sch.dma("sync", tap_ap[:].rearrange("p c t -> p (c t)"),
                        h2[:], "SO")
            if ffn_stage_barriers:
                sch.phase_barrier()
            for tc in range(4):
                sl = tc % 2
                sx_ac3 = sch.dma(
                    "sync", at[:, sl * 1024:(sl + 1) * 1024],
                    accs[p * 512 + tc * 128: p * 512 + (tc + 1) * 128, :], "SX",
                    waits=[("SX", sx_acw[15] if p == 0 and tc == 0 else 0),
                           ("SV", sv_f3.get((p, tc - 2, 1), 0)),
                           ("SO", so_out)])
                for fh in range(2):
                    for f2c in range(34):
                        w = []
                        if f2c == 0:
                            w = [("SA", sa_rs2), ("SV", sv_rs2), ("SG", sg_w2),
                                 ("SV", sv_f3.get((p, tc - 1, fh), 0)
                                  if tc >= 1 else sv_f3.get((p - 1, 3, fh), 0))]
                        if f2c < 32:
                            sp_f3[(p, tc, fh)] = sch.op(
                                "tensor",
                                lambda e, fh=fh, f2c=f2c, tc=tc: e.matmul(
                                    psF3[:, fh * 512:(fh + 1) * 512],
                                    h2[:, f2c * 512 + tc * 128:
                                       f2c * 512 + (tc + 1) * 128],
                                    W2sb[:, f2c * 1024 + fh * 512:
                                         f2c * 1024 + (fh + 1) * 512],
                                    start=(f2c == 0), stop=False),
                                inc=None, waits=w)
                        else:
                            lhs = rs2_hi if f2c == 32 else rs2_lo
                            sp_f3[(p, tc, fh)] = sch.op(
                                "tensor",
                                lambda e, fh=fh, tc=tc, lhs=lhs, f2c=f2c:
                                e.matmul(
                                    psF3[:, fh * 512:(fh + 1) * 512],
                                    lhs[0:1, tc * 128:(tc + 1) * 128],
                                    negh_bf[0:1, 0:512],
                                    start=False, stop=(f2c == 33)),
                                inc="SP" if f2c == 33 else None, waits=w)
                    sv_f3[(p, tc, fh)] = sch.op(
                        "vector",
                        lambda e, sl=sl, fh=fh: e.scalar_tensor_tensor(
                            ob[:, sl * 1024 + fh * 512:
                               sl * 1024 + (fh + 1) * 512],
                            psF3[:, fh * 512:(fh + 1) * 512], scl[:, 2:3],
                            at[:, sl * 1024 + fh * 512:
                               sl * 1024 + (fh + 1) * 512],
                            ALU.mult, ALU.add),
                        inc="SV",
                        waits=[("SP", sp_f3[(p, tc, fh)]), ("SX", sx_ac3)])
                sv_fen = sch.op(
                    "vector", lambda e: e.tensor_copy(fen_v[:], ob[:, 0:8]),
                    inc="SV", waits=[("SV", sv_f3[(p, tc, 1)])])
                so_out = sch.dma(
                    "sync",
                    out[p * 512 + tc * 128: p * 512 + (tc + 1) * 128, :],
                    ob[:, sl * 1024:(sl + 1) * 1024], "SO",
                    waits=[("SV", sv_fen)])
            prev_pass_sp = sch.cnt["SP"]

        sch.wait("sync", ("SO", sch.cnt["SO"]))
        sch.flush(block)
        pc.close()

    stack.close()
    return nc



def _build_abias(role):
    ab = np.full((H, 5, NDT, 128, 128), NEG, np.float32)
    ik = np.arange(128)[:, None]
    jq = np.arange(128)[None, :]
    for head in range(H):
        slope = SLOPES[head]
        for c in range(5):
            for dt in range(NDT):
                delta = jq - ik + 512 - 128 * dt
                valid = (delta >= 0) & (delta <= 512)
                if role == 0 and c < 4 and (c + dt) < 4:
                    valid = np.zeros_like(valid)
                if role == 1 or c == 4:
                    M = slope * 512.0 + C_STAB
                else:
                    M = slope * np.minimum(512.0, 128.0 * c + jq) + C_STAB
                ab[head, c, dt] = np.where(valid, slope * delta - M, NEG)
    return ab


def make_core_inputs(x, w_in, w_out, k0, k1, k2, s0, s1, s2):
    x = np.asarray(x)
    w_in = np.ascontiguousarray(np.asarray(w_in))
    w_out = np.ascontiguousarray(np.asarray(w_out))
    k0 = np.asarray(k0).astype(np.uint8)
    k1 = np.asarray(k1).astype(np.uint8)
    k2 = np.asarray(k2).astype(np.uint8)

    w_inT = np.ascontiguousarray(w_in.T)
    wg = np.empty((4, 8, 128, 768), np.float32)
    for g in range(4):
        for ec in range(8):
            rows = w_inT[ec * 128:(ec + 1) * 128]
            wg[g, ec, :, 0:256] = rows[:, 256 * g: 256 * g + 256]
            wg[g, ec, :, 256:512] = rows[:, 1024 + 256 * g: 1024 + 256 * g + 256]
            wg[g, ec, :, 512:768] = rows[:, 2048 + 256 * g: 2048 + 256 * g + 256]
    wo = np.ascontiguousarray(w_out.T.reshape(8, 128, 1024))
    k0r = np.ascontiguousarray(k0.reshape(16, 128, F1))
    k1r = np.ascontiguousarray(
        k1.reshape(32, 128, 32, 128).transpose(2, 1, 0, 3))
    k2r = np.ascontiguousarray(k2.reshape(32, 128, 1024))
    svec = np.ascontiguousarray(
        np.broadcast_to(
            np.array([np.float32(s0), np.float32(s1), np.float32(s2)],
                     np.float32)[None, :], (128, 3)))
    eye = np.eye(128, dtype=ml_dtypes.bfloat16)
    ab = [_build_abias(0), _build_abias(1)]

    ins = []
    for c in range(8):
        b, role = c // 2, c % 2
        xcr = np.empty((TOKH, E), np.float32)
        if role == 0:
            xcr[:HALO] = 0.0
            xcr[HALO:] = x[b, 0:TOK]
        else:
            xcr[:HALO] = x[b, TOK - HALO:TOK]
            xcr[HALO:] = x[b, TOK:]
        ins.append(dict(xc=np.ascontiguousarray(xcr), wg=wg, wo=wo, k0=k0r,
                        k1r=k1r, k2r=k2r, svec=svec, abias=ab[role], eye=eye))
    return ins


def kernel(L0_attn_x, L0_attn_w_in, L0_attn_w_out, L0_ffn_k0, L0_ffn_k1,
           L0_ffn_k2, L0_ffn_s0, L0_ffn_s1, L0_ffn_s2):
    ins = make_core_inputs(L0_attn_x, L0_attn_w_in, L0_attn_w_out, L0_ffn_k0,
                           L0_ffn_k1, L0_ffn_k2, L0_ffn_s0, L0_ffn_s1,
                           L0_ffn_s2)
    nc = build_program()
    res = run_bass_kernel_spmd(nc, ins, core_ids=list(range(8)))
    out = np.empty((B, S, E), np.float32)
    for c in range(8):
        b, role = c // 2, c % 2
        out[b, role * TOK:(role + 1) * TOK] = res.results[c]["out"]
    return out


# version 19
# speedup vs baseline: 1.0542x; 1.0509x over previous
import numpy as np
import ml_dtypes
from contextlib import ExitStack

import concourse.bass as bass
import concourse.mybir as mybir
from concourse.bass_utils import run_bass_kernel_spmd

F32 = mybir.dt.float32
F16 = mybir.dt.float16
BF16 = mybir.dt.bfloat16
U8 = mybir.dt.uint8
AF = mybir.ActivationFunctionType
ALU = mybir.AluOpType
AX = mybir.AxisListType

B, S, E = 4, 4096, 1024
H, D, WIN = 16, 64, 512
TOK = 2048
HALO = 512
TOKH = TOK + HALO
NCH = TOKH // 128
NQT = 16
NDT = 5
C_STAB = 10.0
SLOPES = [2.0 ** (-(h + 1) * 0.5) for h in range(H)]
F1 = 4096
NEG = -1.0e30


class Sch:
    ENGS = ("sync", "gpsimd", "tensor", "vector", "scalar")

    def __init__(self, nc, stack):
        self.nc = nc
        self.stack = stack
        self.ops = {e: [] for e in self.ENGS}
        self.sems = {}
        self.cnt = {}

    def sem(self, name):
        h = self.stack.enter_context(self.nc.semaphore(name))
        self.sems[name] = h
        self.cnt[name] = 0

    def op(self, eng, fn, inc=None, k=1, waits=()):
        w = {}
        for s, v in waits:
            if v and v > 0:
                w[s] = max(w.get(s, 0), v)
        n = None
        if inc is not None:
            self.cnt[inc] += k
            n = self.cnt[inc]
        sems = self.sems

        def thunk(e, fn=fn, w=tuple(w.items()), inc=inc, k=k):
            for s, v in w:
                e.wait_ge(sems[s], v)
            i = fn(e)
            if inc is not None:
                i.then_inc(sems[inc], k)

        self.ops[eng].append(thunk)
        return n

    def dma(self, eng, out, in_, inc, waits=()):
        return self.op(eng, lambda e: e.dma_start(out, in_), inc=inc, k=16,
                       waits=waits)

    def wait(self, eng, *waits):
        w = [(s, v) for (s, v) in waits if v and v > 0]
        if not w:
            return
        sems = self.sems

        def thunk(e, w=tuple(w)):
            for s, v in w:
                e.wait_ge(sems[s], v)

        self.ops[eng].append(thunk)

    def phase_barrier(self, engs=None):
        snap = [(s, self.cnt[s]) for s in self.sems]
        for eng in (engs or self.ENGS):
            self.wait(eng, *snap)

    def flush(self, block):
        for eng in self.ENGS:
            ops = self.ops[eng]
            if not ops:
                continue
            self.ops[eng] = []

            def body(e, ops=ops):
                for f in ops:
                    f(e)
            getattr(block, eng)(body)


def build_program(tap=None, stop_after=None, att_heads=4, qkv_parts="qkv",
                  detect_races=True, ffn2_serial=False, ffn2_nocorr=False,
                  ffn_stage_barriers=False):
    nc = bass.Bass("TRN2", target_bir_lowering=False,
                   detect_race_conditions=detect_races)
    stack = ExitStack()

    xc = nc.dram_tensor("xc", [TOKH, E], F32, kind="ExternalInput").ap()
    wg = nc.dram_tensor("wg", [4, 8, 128, 768], F32, kind="ExternalInput").ap()
    wo = nc.dram_tensor("wo", [8, 128, 1024], F32, kind="ExternalInput").ap()
    k0 = nc.dram_tensor("k0", [16, 128, F1], U8, kind="ExternalInput").ap()
    k1r = nc.dram_tensor("k1r", [32, 128, 32, 128], U8, kind="ExternalInput").ap()
    k2r = nc.dram_tensor("k2r", [32, 128, 1024], U8, kind="ExternalInput").ap()
    svec = nc.dram_tensor("svec", [128, 3], F32, kind="ExternalInput").ap()
    abias = nc.dram_tensor("abias", [5, NDT, 128, 128], F16,
                           kind="ExternalInput").ap()
    eye = nc.dram_tensor("eye", [128, 128], BF16, kind="ExternalInput").ap()
    out = nc.dram_tensor("out", [TOK, E], F32, kind="ExternalOutput").ap()

    w0s = nc.dram_tensor("w0s", [32, 128, 1024], BF16, kind="Internal").ap()
    xns = nc.dram_tensor("xns", [8, 128, TOK], BF16, kind="Internal").ap()
    accs = nc.dram_tensor("accs", [TOK, E], F32, kind="Internal").ap()
    dend = nc.dram_tensor("dend", [H, NQT, 128], F32, kind="Internal").ap()

    tap_ap = None
    if tap == "xn":
        tap_ap = nc.dram_tensor("tap", [8, 128, TOKH], BF16, kind="ExternalOutput").ap()
    elif tap == "q":
        tap_ap = nc.dram_tensor("tap", [128, 2, TOK], BF16, kind="ExternalOutput").ap()
    elif tap == "k":
        tap_ap = nc.dram_tensor("tap", [128, 2, TOKH], BF16, kind="ExternalOutput").ap()
    elif tap == "v":
        tap_ap = nc.dram_tensor("tap", [128, NCH * 4 * 65], BF16, kind="ExternalOutput").ap()
    elif tap == "attn":
        tap_ap = nc.dram_tensor("tap", [128, 8, TOK], BF16, kind="ExternalOutput").ap()
    elif tap == "acc":
        tap_ap = nc.dram_tensor("tap", [TOK, E], F32, kind="ExternalOutput").ap()
    elif tap in ("h1", "h2"):
        tap_ap = nc.dram_tensor("tap", [128, 32, 512], BF16, kind="ExternalOutput").ap()

    sch = Sch(nc, stack)
    for s in ("SX", "SY", "SG", "SP", "SA", "SV", "SPL", "SO"):
        sch.sem(s)

    sb = lambda *a: stack.enter_context(nc.sbuf_tensor(*a))

    eye_sb = sb("eye_sb", [128, 128], BF16)
    svec_sb = sb("svec_sb", [128, 3], F32)
    scl = sb("scl", [128, 3], F32)
    ones_pe = sb("ones_pe", [128, 1], BF16)
    cstab = sb("cstab", [128, 1], F32)
    oneq = sb("oneq", [1, 128], F32)
    negh = sb("negh", [1, 512], F32)
    negh_bf = sb("negh_bf", [1, 512], BF16)
    rs1_sb = sb("rs1_sb", [1, 512], F32)
    rs2_sb = sb("rs2_sb", [1, 512], F32)
    rs1_hi = sb("rs1_hi", [1, 512], BF16)
    rs1_lo = sb("rs1_lo", [1, 512], BF16)
    rs2_hi = sb("rs2_hi", [1, 512], BF16)
    rs2_lo = sb("rs2_lo", [1, 512], BF16)
    at = sb("at", [128, 2 * 1024], F32)
    ob = sb("ob", [128, 2 * 1024], F32)
    fen_v = sb("fen_v", [128, 8], F32)
    fen_a = sb("fen_a", [128, 8], F32)
    fen_g = sb("fen_g", [128, 8], BF16)

    with nc.Block() as block:
        p0 = ExitStack()
        sb0 = lambda *a: p0.enter_context(nc.sbuf_tensor(*a))
        xn_stack = ExitStack()
        xn = xn_stack.enter_context(nc.sbuf_tensor("xn", [128, 8 * TOKH], BF16))

        def xn_v(ec, t0, n):
            return xn[:, ec * TOKH + t0: ec * TOKH + t0 + n]

        xtok = sb0("xtok", [128, 2 * 1024], F32)
        xnt = sb0("xnt", [128, 2 * 1024], BF16)
        sq = sb0("sq", [128, 1024], F32)
        xsum = sb0("xsum", [128, NCH], F32)
        xsqs = sb0("xsqs", [128, NCH], F32)
        tmc = sb0("tmc", [128, 2], F32)
        tve = sb0("tve", [128, 2], F32)
        tms = sb0("tms", [128, 2], F32)
        tsd = sb0("tsd", [128, 2], F32)
        trs = sb0("trs", [128, 2], F32)
        k0t = sb0("k0t", [128, 2 * F1], BF16)
        k0b = sb0("k0b", [128, 2 * F1], BF16)
        w0c = sb0("w0c", [128, 2 * F1], BF16)
        pst = p0.enter_context(nc.psum_tensor("pst", [128, 2048], BF16))

        sx_eye = sch.dma("sync", eye_sb[:], eye, "SX")
        sx_sv = sch.dma("sync", svec_sb[:], svec, "SX")
        sch.op("gpsimd", lambda e: e.memset(ones_pe[:], 1.0))
        sch.op("gpsimd", lambda e: e.memset(cstab[:], -C_STAB))
        sch.op("gpsimd", lambda e: e.memset(oneq[:], 1.0))
        sch.op("gpsimd", lambda e: e.memset(negh[:], -0.5))
        sch.op("gpsimd", lambda e: e.memset(negh_bf[:], -0.5))
        spl_ms = sch.op("gpsimd", lambda e: e.memset(xsqs[:], 0.0), inc="SPL")
        spl_init = spl_ms
        sv_scl = None
        for j, f in ((0, 2.0), (1, 4.0), (2, 4.0)):
            sv_scl = sch.op(
                "vector",
                lambda e, j=j, f=f: e.tensor_scalar_mul(
                    scl[:, j:j + 1], svec_sb[:, j:j + 1], f),
                inc="SV", waits=[("SX", sx_sv)])

        sv_stt = [0] * NCH
        sa_sq = [0] * NCH
        sp_tr = [0] * NCH
        sa_ev = [0] * NCH
        c1 = 1.0 / 1024.0

        spl_sub = [0] * 8
        sx_w0 = [0] * 8
        sg_k0 = [0] * 8

        def emit_k0(ec):
            sl = ec % 2
            sch.dma("gpsimd", k0t[:, sl * F1:(sl + 1) * F1], k0[ec], "SG",
                    waits=[("SPL", spl_sub[ec - 2] if ec >= 2 else 0)])
            sg_k0[ec] = sch.dma("gpsimd", k0b[:, sl * F1:(sl + 1) * F1],
                                k0[ec + 8], "SG")
            spl_sub[ec] = sch.op(
                "gpsimd",
                lambda e, sl=sl: e.tensor_sub(w0c[:, sl * F1:(sl + 1) * F1],
                                              k0t[:, sl * F1:(sl + 1) * F1],
                                              k0b[:, sl * F1:(sl + 1) * F1]),
                inc="SPL",
                waits=[("SG", sg_k0[ec]), ("SG", sx_w0[ec - 2] if ec >= 2 else 0)])
            spl_sub[ec] = sch.op(
                "gpsimd",
                lambda e, sl=sl: e.tensor_copy(fen_g[:],
                                               w0c[:, sl * F1: sl * F1 + 8]),
                inc="SPL", waits=[("SPL", spl_sub[ec])])
            sx_w0[ec] = sch.dma(
                "gpsimd",
                w0s[:, :, ec * 128:(ec + 1) * 128].rearrange(
                    "f p t -> p f t"),
                w0c[:, sl * F1:(sl + 1) * F1].rearrange(
                    "p (f t) -> p f t", f=32), "SG")

        for tc in range(NCH):
            sl = tc % 2
            xq, xsem = (("sync", "SX") if tc % 2 == 0 else ("scalar", "SY"))
            sx_x = sch.dma(
                xq, xtok[:, sl * 1024:(sl + 1) * 1024],
                xc[tc * 128:(tc + 1) * 128, :], xsem,
                waits=[("SV", sv_stt[tc - 2] if tc >= 2 else 0),
                       ("SA", sa_sq[tc - 2] if tc >= 2 else 0)])
            xt = xtok[:, sl * 1024:(sl + 1) * 1024]
            sch.op("vector",
                   lambda e, xt=xt, tc=tc: e.reduce_sum(
                       xsum[:, tc:tc + 1], xt, axis=AX.X),
                   inc="SV", waits=[(xsem, sx_x)])
            sa_sq[tc] = sch.op(
                "scalar",
                lambda e, xt=xt, tc=tc: e.activation(
                    sq[:], xt, AF.Square, accum_out=xsqs[:, tc:tc + 1]),
                inc="SA", waits=[(xsem, sx_x), ("SPL", spl_ms)])
            sa_sq[tc] = sch.op(
                "scalar",
                lambda e, tc=tc: e.copy(fen_a[0:1, 0:1], xsqs[0:1, tc:tc + 1]),
                inc="SA", waits=[("SA", sa_sq[tc])])
            sch.op("vector",
                   lambda e, tc=tc, sl=sl: e.tensor_scalar(
                       tve[:, sl:sl + 1], xsqs[:, tc:tc + 1], c1, 1e-5,
                       ALU.mult, ALU.add),
                   inc="SV", waits=[("SA", sa_sq[tc])])
            sv_m = sch.op("vector",
                          lambda e, tc=tc, sl=sl: e.tensor_scalar_mul(
                              tmc[:, sl:sl + 1], xsum[:, tc:tc + 1], c1),
                          inc="SV")
            sv_ms = sch.op("vector",
                           lambda e, sl=sl: e.tensor_mul(
                               tms[:, sl:sl + 1], tmc[:, sl:sl + 1],
                               tmc[:, sl:sl + 1]),
                           inc="SV", waits=[("SV", sv_m)])
            sv_ve = sch.op("vector",
                           lambda e, sl=sl: e.tensor_sub(
                               tve[:, sl:sl + 1], tve[:, sl:sl + 1],
                               tms[:, sl:sl + 1]), inc="SV",
                           waits=[("SV", sv_ms)])
            sa_sd = sch.op("scalar",
                           lambda e, sl=sl: e.sqrt(tsd[:, sl:sl + 1],
                                                   tve[:, sl:sl + 1]),
                           inc="SA", waits=[("SV", sv_ve)])
            sa_sd = sch.op("scalar",
                           lambda e, sl=sl: e.copy(fen_a[0:1, 0:1],
                                                   tsd[0:1, sl:sl + 1]),
                           inc="SA", waits=[("SA", sa_sd)])
            sv_rc = sch.op("vector",
                           lambda e, sl=sl: e.reciprocal(trs[:, sl:sl + 1],
                                                         tsd[:, sl:sl + 1]),
                           inc="SV", waits=[("SA", sa_sd)])
            xnv = xnt[:, sl * 1024:(sl + 1) * 1024]
            sv_stt[tc] = sch.op(
                "vector",
                lambda e, xt=xt, xnv=xnv, sl=sl: e.tensor_scalar(
                    xnv, xt, tmc[:, sl:sl + 1], trs[:, sl:sl + 1],
                    ALU.subtract, ALU.mult),
                inc="SV", waits=[("SP", sp_tr[tc - 2] if tc >= 2 else 0),
                                 ("SV", sv_rc)])
            sv_stt[tc] = sch.op(
                "vector",
                lambda e, xnv=xnv: e.tensor_copy(fen_v[:], xnv[:, 0:8]),
                inc="SV", waits=[("SV", sv_stt[tc])])
            pslot = pst[:, (tc % 2) * 1024:(tc % 2) * 1024 + 1024]
            for j in range(8):
                sp_tr[tc] = sch.op(
                    "tensor",
                    lambda e, xnv=xnv, j=j, pslot=pslot: e.transpose(
                        pslot[:, j * 128:(j + 1) * 128],
                        xnv[:, j * 128:(j + 1) * 128], eye_sb[:]),
                    inc="SP",
                    waits=([("SV", sv_stt[tc]), ("SX", sx_eye),
                            ("SA", sa_ev[tc - 2] if tc >= 2 else 0)]
                           if j == 0 else []))
            xn_dst = xn[:].rearrange("p (c t) -> p c t", c=8)[
                :, :, tc * 128:(tc + 1) * 128]
            pst_v = pslot.rearrange("p (c t) -> p c t", c=8)
            sa_ev[tc] = sch.op(
                "scalar",
                lambda e, xn_dst=xn_dst, pst_v=pst_v: e.copy(xn_dst, pst_v),
                inc="SA", waits=[("SP", sp_tr[tc])])
            if tc >= 4 and tc % 2 == 0 and tc // 2 - 2 < 8:
                emit_k0(tc // 2 - 2)

        sa_fence = sch.op("scalar",
                          lambda e: e.copy(fen_a[:], xn[:, 0:8]),
                          inc="SA", waits=[("SA", sa_ev[NCH - 1])])
        sx_xns = 0
        for ec in range(8):
            sx_xns = sch.dma("sync", xns[ec], xn_v(ec, HALO, TOK), "SX",
                             waits=[("SA", sa_fence)])
        if tap == "xn":
            for ec in range(8):
                sch.dma("sync", tap_ap[ec], xn_v(ec, 0, TOKH), "SO",
                        waits=[("SA", sa_ev[NCH - 1])])

        if stop_after == "p0":
            sch.wait("sync", ("SO", sch.cnt["SO"]), ("SX", sch.cnt["SX"]))
            sch.flush(block)
            p0.close()
            xn_stack.close()
            stack.close()
            return nc
        sch.phase_barrier()
        sch.flush(block)
        p0.close()

        pb = ExitStack()
        sbB = lambda *a: pb.enter_context(nc.sbuf_tensor(*a))
        psB_ = lambda *a: pb.enter_context(nc.psum_tensor(*a))
        q_sb = sbB("q_sb", [128, 2, TOK], BF16)
        k_sb = sbB("k_sb", [128, 2, TOKH], BF16)
        v_sb = sbB("v_sb", [128, NCH, 4, 65], BF16)
        o_sb = sbB("o_sb", [128, 8, TOK], BF16)
        wgb = sbB("wgb", [128, 2, 8, 768], BF16)
        ab_sb = sbB("ab_sb", [128, 5, NDT * 128], F16)
        p_sb = sbB("p_sb", [128, 3, NDT * 128], BF16)
        den_s = sbB("den_s", [1, 2 * 128], F32)
        den_all = sbB("den_all", [128, 256], F32)
        db_row = sbB("db_row", [1, 2 * 2048], F32)
        Dsb = sbB("Dsb", [128, 2 * 1024], F32)
        psS3 = psB_("psS3", [128, 3072], F32)
        psPV = psB_("psPV", [128, 1024], F32)

        sg_wg = [0] * 4
        sp_qkv = [0] * 4
        sp_att = [0] * 4
        sv_att = [0] * 4
        sa_att = [0] * 4
        it = 0
        sa_exp = {}
        sp_pv = {}
        sp_sc = {}
        sa_oev = {}
        sx_den = {}
        sp_nb = {}
        sa_nb = {}
        sv_nb = {}
        sv_ab = {}
        spl_ab = {}
        sv_vev = [0] * 4
        sa_qk = [0] * 4

        sx_ab = 0
        for c in range(5):
            sx_ab = sch.dma(
                "sync", ab_sb[:, c].rearrange("p (d q) -> p d q", d=NDT),
                abias[c].rearrange("d p q -> p d q"), "SX")

        for g in range(4):
            gs = g % 2
            for ec in range(8):
                n = sch.dma(
                    "gpsimd", wgb[:, gs, ec, :], wg[g, ec], "SG",
                    waits=([("SP", sp_att[g - 2] if g >= 2 else 0)]
                           if ec == 0 else []))
            sg_wg[g] = n
            spl_v1 = sch.op(
                "gpsimd", lambda e: e.memset(v_sb[:, :, :, 64:65], 1.0),
                inc="SPL", waits=[("SP", sp_att[g - 1] if g >= 1 else 0)])
            qkv_entry = [("SG", sg_wg[g]), ("SP", sp_att[g - 1] if g else 0),
                         ("SA", sa_att[g - 1] if g else 0),
                         ("SV", sv_att[g - 1] if g else 0)]
            pp = 0
            sa_last_evac = {0: 0, 1: 0}
            for mc in range(2):
                for what, nspan, cols in (("q", 4, 0), ("k", 5, 256)):
                    if what not in qkv_parts:
                        continue
                    for span in range(nspan):
                        pv = psS3[:, pp * 1024: pp * 1024 + 512]
                        t0 = (HALO if what == "q" else 0) + span * 512
                        ready = sa_ev[min(NCH - 1, t0 // 128 + 3)]
                        for ec in range(8):
                            w = (list(qkv_entry) + [("SA", sa_last_evac[pp]),
                                                    ("SA", ready)]
                                 if ec == 0 else [])
                            n = sch.op(
                                "tensor",
                                lambda e, pv=pv, ec=ec, mc=mc, cols=cols,
                                t0=t0, gs=gs:
                                e.matmul(
                                    pv,
                                    wgb[:, gs, ec, cols + mc * 128: cols + (mc + 1) * 128],
                                    xn_v(ec, t0, 512),
                                    start=(ec == 0), stop=(ec == 7)),
                                inc="SP" if ec == 7 else None, waits=w)
                        sp_qkv[g] = n
                        if what == "q":
                            dst = q_sb[:, mc, t0 - HALO: t0 - HALO + 512]
                            scale = 0.125
                        else:
                            dst = k_sb[:, mc, t0: t0 + 512]
                            scale = 1.0
                        sa_qk[g] = sa_last_evac[pp] = sch.op(
                            "scalar",
                            lambda e, pv=pv, dst=dst, scale=scale:
                            e.activation(dst, pv, AF.Copy, scale=scale),
                            inc="SA", waits=[("SP", sp_qkv[g])])
                        pp ^= 1
            sa_qk[g] = sch.op(
                "scalar", lambda e: e.copy(fen_a[:], k_sb[:, 0, 0:8]),
                inc="SA", waits=[("SA", sa_qk[g])])
            vpp = 0
            sv_last_vev = {0: 0, 1: 0}
            nv_ch = NCH if "v" in qkv_parts else (2 if "w" in qkv_parts else 0)
            for tc in range(nv_ch):
                pv = psS3[:, 2048 + vpp * 512: 2048 + vpp * 512 + 256]
                for ec in range(8):
                    w = ([("SPL", spl_v1), ("SV", sv_last_vev[vpp]),
                          ("SA", sa_ev[tc])]
                         + (qkv_entry if tc == 0 else []) if ec == 0 else [])
                    n = sch.op(
                        "tensor",
                        lambda e, pv=pv, ec=ec, tc=tc, gs=gs: e.matmul(
                            pv, xn_v(ec, tc * 128, 128), wgb[:, gs, ec, 512:768],
                            start=(ec == 0), stop=(ec == 7)),
                        inc="SP" if ec == 7 else None, waits=w)
                sp_qkv[g] = n
                pvv = pv.rearrange("p (h d) -> p h d", h=4)
                sv_vev[g] = sv_last_vev[vpp] = sch.op(
                    "vector",
                    lambda e, pvv=pvv, tc=tc: e.tensor_copy(
                        v_sb[:, tc, :, 0:64], pvv),
                    inc="SV", waits=[("SP", sp_qkv[g])])
                vpp ^= 1
            sv_vev[g] = sch.op(
                "vector",
                lambda e: e.tensor_copy(fen_v[:], v_sb[:, 0, 0, 0:8]),
                inc="SV", waits=[("SV", sv_vev[g])])

            def emit_pv(ctx):
                cit, chh, cpo, cjg, chead, cqt, cslot = ctx
                vslot = cit % 2
                pvs = psPV[:, vslot * 512: vslot * 512 + 128]
                for dt in range(NDT):
                    kc = cqt + dt
                    w = [("SA", sa_exp[cit])]
                    if dt == 0:
                        w.append(("SA", sa_oev.get(cit - 2, 0)))
                    sp_pv[cit] = sch.op(
                        "tensor",
                        lambda e, pvs=pvs, kc=kc, chh=chh, cslot=cslot, dt=dt:
                        e.matmul(
                            pvs[0:65, :], v_sb[:, kc, chh, :],
                            p_sb[:, cslot, dt * 128:(dt + 1) * 128],
                            start=(dt == 0), stop=(dt == NDT - 1)),
                        inc="SP" if dt == NDT - 1 else None, waits=w)
                sch.op(
                    "scalar",
                    lambda e, pvs=pvs, cpo=cpo, cjg=cjg, cqt=cqt: e.copy(
                        o_sb[cpo:cpo + 64, cjg, cqt * 128:(cqt + 1) * 128],
                        pvs[0:64, :]),
                    inc="SA", waits=[("SP", sp_pv[cit])])
                sa_oev[cit] = sch.op(
                    "scalar",
                    lambda e, pvs=pvs, vslot=vslot: e.copy(
                        den_s[0:1, vslot * 128:(vslot + 1) * 128],
                        pvs[64:65, :]),
                    inc="SA",
                    waits=[("SX", sx_den.get(cit - 2, 0))])
                sa_oev[cit] = sch.op(
                    "scalar", lambda e: e.copy(fen_a[0:1, :], den_s[0:1, 0:8]),
                    inc="SA", waits=[("SA", sa_oev[cit])])
                sx_den[cit] = sch.dma(
                    "sync", dend[chead, cqt],
                    den_s[0:1, vslot * 128:(vslot + 1) * 128], "SX",
                    waits=[("SA", sa_oev[cit])])

            sv_adds = {}

            def emit_exp(ctx):
                cit, chh, cpo, cjg, chead, cqt, cslot = ctx
                psSl = psS3[:, cslot * 1024: cslot * 1024 + 640]
                sa_exp[cit] = sch.op(
                    "scalar",
                    lambda e, psSl=psSl, cslot=cslot: e.activation(
                        p_sb[:, cslot, :], psSl, AF.Exp, bias=cstab[:]),
                    inc="SA",
                    waits=[("SV", sv_adds[cit]), ("SP", sp_pv.get(cit - 3, 0))])

            q_exp = []
            q_pv = []
            for hh in range(att_heads):
                head = 4 * g + hh
                po = 64 * (hh % 2)
                mc = hh // 2
                jg = 2 * g + mc
                slope = SLOPES[head]
                for qt in range(NQT):
                    cls = min(qt, 4)
                    slot = it % 3
                    psS = psS3[:, slot * 1024: slot * 1024 + 640]
                    for dt in range(NDT):
                        kc = qt + dt
                        w = []
                        if dt == 0:
                            w = [("SA", sa_exp.get(it - 3, 0)),
                                 ("SA", sa_qk[g]), ("SV", sv_vev[g])]
                        n = sch.op(
                            "tensor",
                            lambda e, psS=psS, dt=dt, po=po, mc=mc, kc=kc, qt=qt:
                            e.matmul(
                                psS[:, dt * 128:(dt + 1) * 128],
                                k_sb[po:po + 64, mc, kc * 128:(kc + 1) * 128],
                                q_sb[po:po + 64, mc, qt * 128:(qt + 1) * 128],
                                start=True, stop=True),
                            inc="SP" if dt == NDT - 1 else None, waits=w)
                    sp_sc[it] = n
                    sv_adds[it] = sch.op(
                        "vector",
                        lambda e, psS=psS, cls=cls, slope=slope:
                        e.scalar_tensor_tensor(
                            psS, ab_sb[:, cls], slope, psS,
                            ALU.mult, ALU.add),
                        inc="SV",
                        waits=[("SP", sp_sc[it]), ("SX", sx_ab)])
                    ctx = (it, hh, po, jg, head, qt, slot)
                    q_exp.append(ctx)
                    if len(q_exp) > 1:
                        emit_exp(q_exp.pop(0))
                    q_pv.append(ctx)
                    if len(q_pv) > 2:
                        emit_pv(q_pv.pop(0))
                    it += 1
            for ctx in q_exp:
                emit_exp(ctx)
            while q_pv:
                emit_pv(q_pv.pop(0))
            sp_att[g] = sch.cnt["SP"]
            sv_att[g] = sch.cnt["SV"]
            sa_att[g] = sch.cnt["SA"]

        sch.op("scalar", lambda e: e.copy(fen_a[:], o_sb[:, 0, 0:8]),
               inc="SA", waits=[("SA", sch.cnt["SA"])])
        att_sa = sch.cnt["SA"]
        att_sx = sch.cnt["SX"]
        sx_da = sch.dma("sync", den_all[:],
                        dend[:].rearrange("h (a b) d -> (h a) (b d)", b=2), "SX",
                        waits=[("SX", att_sx)])
        sv_da = sch.op("vector", lambda e: e.reciprocal(den_all[:], den_all[:]),
                       inc="SV", waits=[("SX", sx_da)])
        sv_da = sch.op("vector", lambda e: e.tensor_copy(fen_v[:], den_all[:, 0:8]),
                       inc="SV", waits=[("SV", sv_da)])
        sx_dw = sch.dma("sync",
                        dend[:].rearrange("h (a b) d -> (h a) (b d)", b=2),
                        den_all[:], "SX", waits=[("SV", sv_da)])
        sx_dbr = {}
        for head in range(H if att_heads == 4 else 0):
            po = 64 * (head % 2)
            jg = head // 2
            hs = head % 2
            sx_dbr[head] = sch.dma(
                "sync", db_row[0:1, hs * 2048:(hs + 1) * 2048],
                dend[head].rearrange("q d -> (q d)"), "SX",
                waits=[("SX", sx_dw), ("SP", sp_nb.get(2 * (head - 2) + 1, 0))])
            for j in range(2):
                gq = head * 2 + j
                slot = gq % 2
                psN = psS3[:, slot * 1024:(slot + 1) * 1024]
                w = [("SX", sx_dbr[head]), ("SA", sa_nb.get(gq - 2, 0)),
                     ("SPL", spl_init if gq == 0 else 0), ("SA", att_sa)]
                for half in range(2):
                    sp_nb[gq] = sch.op(
                        "tensor",
                        lambda e, psN=psN, hs=hs, j=j, half=half: e.matmul(
                            psN[:, half * 512:(half + 1) * 512], oneq[:],
                            db_row[0:1, hs * 2048 + j * 1024 + half * 512:
                                   hs * 2048 + j * 1024 + (half + 1) * 512],
                            start=True, stop=True),
                        inc="SP" if half == 1 else None,
                        waits=(w if half == 0 else []))
                sa_nb[gq] = sch.op(
                    "scalar",
                    lambda e, psN=psN, slot=slot: e.copy(
                        Dsb[:, slot * 1024:(slot + 1) * 1024], psN),
                    inc="SA",
                    waits=[("SP", sp_nb[gq]), ("SV", sv_nb.get(gq - 2, 0))])
                sv_nb[gq] = sch.op(
                    "vector",
                    lambda e, po=po, jg=jg, j=j, slot=slot: e.tensor_mul(
                        o_sb[po:po + 64, jg, j * 1024:(j + 1) * 1024],
                        o_sb[po:po + 64, jg, j * 1024:(j + 1) * 1024],
                        Dsb[po:po + 64, slot * 1024:(slot + 1) * 1024]),
                    inc="SV", waits=[("SA", sa_nb[gq])])
        sch.op("vector", lambda e: e.tensor_copy(fen_v[:], o_sb[:, 0, 0:8]),
               inc="SV", waits=[("SV", sch.cnt["SV"])])
        sv_norm = sch.cnt["SV"]
        norm_sa = sch.cnt["SA"]

        if tap == "q":
            sch.wait("sync", ("SA", sch.cnt["SA"]))
            for mcx in range(2):
                sch.dma("sync", tap_ap[:, mcx, :], q_sb[:, mcx, :], "SO")
        if tap == "k":
            sch.wait("sync", ("SA", sch.cnt["SA"]))
            for mcx in range(2):
                sch.dma("sync", tap_ap[:, mcx, :], k_sb[:, mcx, :], "SO")
        if tap == "v":
            sch.wait("sync", ("SV", sch.cnt["SV"]))
            sch.dma("sync", tap_ap[:],
                    v_sb[:].rearrange("p c h d -> p (c h d)"), "SO")
        if stop_after == "qkv":
            sch.wait("sync", ("SO", sch.cnt["SO"]), ("SX", sch.cnt["SX"]),
                     ("SA", sch.cnt["SA"]), ("SV", sch.cnt["SV"]))
            sch.flush(block)
            pb.close()
            xn_stack.close()
            stack.close()
            return nc
        if tap == "attn":
            sch.wait("sync", ("SV", sv_norm))
            for jgx in range(8):
                sch.dma("sync", tap_ap[:, jgx, :], o_sb[:, jgx, :], "SO")

        wob = sbB("wob", [128, 8, 1024], BF16)
        sg_wo = 0
        for ec in range(8):
            sg_wo = sch.dma("gpsimd", wob[:, ec, :], wo[ec], "SG")
        sp_opA = {}
        sv_opadd = {}
        sx_acw = {}
        for tc in range(16):
            sl = tc % 2
            sx_opx = sch.dma(
                "sync", at[:, sl * 1024:(sl + 1) * 1024],
                xc[HALO + tc * 128: HALO + (tc + 1) * 128, :], "SX",
                waits=[("SV", sv_opadd.get(tc - 2, 0)),
                       ("SX", sx_acw.get(tc - 2, 0))])
            for half in range(2):
                pv = psS3[:, half * 512:(half + 1) * 512]
                for ec in range(8):
                    w = []
                    if ec == 0:
                        w = [("SV", sv_norm), ("SG", sg_wo),
                             ("SV", sv_opadd.get(tc - 1, 0)),
                             ("SA", norm_sa)]
                    sp_opA[(tc, half)] = sch.op(
                        "tensor",
                        lambda e, pv=pv, ec=ec, tc=tc, half=half: e.matmul(
                            pv, o_sb[:, ec, tc * 128:(tc + 1) * 128],
                            wob[:, ec, half * 512:(half + 1) * 512],
                            start=(ec == 0), stop=(ec == 7)),
                        inc="SP" if ec == 7 else None, waits=w)
                sv_opadd[tc] = sch.op(
                    "vector",
                    lambda e, pv=pv, sl=sl, half=half: e.tensor_add(
                        at[:, sl * 1024 + half * 512:
                           sl * 1024 + (half + 1) * 512],
                        pv,
                        at[:, sl * 1024 + half * 512:
                           sl * 1024 + (half + 1) * 512]),
                    inc="SV", waits=[("SP", sp_opA[(tc, half)]), ("SX", sx_opx)])
            sv_opadd[tc] = sch.op(
                "vector", lambda e: e.tensor_copy(fen_v[:], at[:, 0:8]),
                inc="SV", waits=[("SV", sv_opadd[tc])])
            sx_acw[tc] = sch.dma(
                "sync", accs[tc * 128:(tc + 1) * 128, :],
                at[:, sl * 1024:(sl + 1) * 1024], "SX",
                waits=[("SV", sv_opadd[tc])])
        if tap == "acc":
            sch.dma("sync", tap_ap[:], accs[:], "SO",
                    waits=[("SX", sx_acw[15])])

        if stop_after == "att":
            sch.wait("sync", ("SO", sch.cnt["SO"]), ("SX", sch.cnt["SX"]))
            sch.flush(block)
            pb.close()
            xn_stack.close()
            stack.close()
            return nc
        sch.phase_barrier()
        sch.flush(block)
        pb.close()
        xn_stack.close()

        pc = ExitStack()
        sbC = lambda *a: pc.enter_context(nc.sbuf_tensor(*a))
        psC = lambda *a: pc.enter_context(nc.psum_tensor(*a))
        h2 = sbC("h2", [128, 32 * 512], BF16)
        W2sb = sbC("W2sb", [128, 32 * 1024], BF16)
        h1 = sbC("h1", [128, 32 * 512], BF16)
        k1b = sbC("k1b", [128, 2 * 4096], BF16)
        w0b = sbC("w0b", [128, 2 * 1024], BF16)
        xnp = sbC("xnp", [128, 8 * 512], BF16)
        den_flush = sbC("den_flush", [1, 128], BF16)
        psF = psC("psF", [128, 1024], F32)
        psF3 = psC("psF3", [128, 1024], F32)
        psRS = psC("psRS", [128, 512], F32)

        sg_w2 = 0
        for f2c in range(32):
            sg_w2 = sch.dma("gpsimd", W2sb[:, f2c * 1024:(f2c + 1) * 1024],
                            k2r[f2c], "SG")

        sp_f1 = {}
        sa_r1 = {}
        sp_f2 = {}
        sa_r2 = {}
        sp_f3 = {}
        sv_f3 = {}
        sa_rs1 = sa_rs2 = 0
        so_out = 0
        prev_pass_sp = 0
        sx_xnp = 0

        for p in range(4):
            w = [("SX", sx_xns), ("SP", prev_pass_sp)]
            for ec in range(8):
                sx_xnp = sch.dma("sync", xnp[:, ec * 512:(ec + 1) * 512],
                                 xns[ec][:, p * 512:(p + 1) * 512], "SX",
                                 waits=(w if ec == 0 else []))
            for f1c in range(32):
                sl = f1c % 2
                sx_w0b = sch.dma(
                    "sync", w0b[:, sl * 1024:(sl + 1) * 1024], w0s[f1c], "SX",
                    waits=[("SG", sx_w0[7] if p == 0 and f1c < 2 else 0),
                           ("SP", sp_f1.get((p, f1c - 2), prev_pass_sp))])
                for ec in range(8):
                    w = []
                    if ec == 0:
                        w = [("SX", sx_w0b), ("SX", sx_xnp),
                             ("SA", sa_r1.get((p, f1c - 2), 0)),
                             ("SA", sa_rs2 if p > 0 else 0),
                             ("SV", sv_f3.get((p - 1, 3, 1), 0))]
                    sp_f1[(p, f1c)] = sch.op(
                        "tensor",
                        lambda e, sl=sl, ec=ec: e.matmul(
                            psF[:, sl * 512:(sl + 1) * 512],
                            w0b[:, sl * 1024 + ec * 128:
                                sl * 1024 + (ec + 1) * 128],
                            xnp[:, ec * 512:(ec + 1) * 512],
                            start=(ec == 0), stop=(ec == 7)),
                        inc="SP" if ec == 7 else None, waits=w)
                sa_r1[(p, f1c)] = sch.op(
                    "scalar",
                    lambda e, sl=sl, f1c=f1c: e.activation(
                        h1[:, f1c * 512:(f1c + 1) * 512],
                        psF[:, sl * 512:(sl + 1) * 512], AF.Relu,
                        scale=scl[:, 0:1]),
                    inc="SA", waits=[("SP", sp_f1[(p, f1c)]), ("SV", sv_scl)])
                for rc in ([f1c - 1] if f1c >= 1 else []):
                    w = [("SA", sa_r1.get((p, rc + 1), sa_r1[(p, rc)]))]
                    if rc == 0:
                        w += [("SA", sa_rs1), ("SPL", spl_init)]
                    sch.op(
                        "tensor",
                        lambda e, rc=rc: e.matmul(
                            psRS[0:1, 0:512], ones_pe[:],
                            h1[:, rc * 512:(rc + 1) * 512],
                            start=(rc == 0), stop=False),
                        inc=None, waits=w)
            if ffn_stage_barriers:
                sch.phase_barrier()
            fa = sch.op("scalar",
                        lambda e: e.copy(fen_a[:],
                                         h1[:, 31 * 512:31 * 512 + 8]),
                        inc="SA", waits=[("SA", sa_r1[(p, 31)])])
            sp_rs1 = sch.op(
                "tensor",
                lambda e: e.matmul(
                    psRS[0:1, 0:512], ones_pe[:],
                    h1[:, 31 * 512:32 * 512],
                    start=False, stop=True),
                inc="SP", waits=[("SA", fa)])
            sa_rs1 = sch.op("scalar",
                            lambda e: e.copy(rs1_sb[:], psRS[0:1, 0:512]),
                            inc="SA", waits=[("SP", sp_rs1)])
            sa_rs1 = sch.op("scalar",
                            lambda e: e.copy(rs1_hi[:], rs1_sb[:]),
                            inc="SA", waits=[("SA", sa_rs1)])
            sa_rs1 = sch.op("scalar",
                            lambda e: e.copy(fen_a[0:1, :], rs1_hi[0:1, 0:8]),
                            inc="SA", waits=[("SA", sa_rs1)])
            sv_rs1 = sch.op("vector",
                            lambda e: e.tensor_sub(rs1_lo[:], rs1_sb[:],
                                                   rs1_hi[:]),
                            inc="SV", waits=[("SA", sa_rs1)])
            sv_rs1 = sch.op("vector",
                            lambda e: e.tensor_copy(fen_v[0:1, :],
                                                    rs1_lo[0:1, 0:8]),
                            inc="SV", waits=[("SV", sv_rs1)])
            for f2c in range(32):
                sl = f2c % 2
                sg_k1 = sch.dma(
                    "gpsimd", k1b[:, sl * 4096:(sl + 1) * 4096],
                    k1r[f2c].rearrange("p c t -> p (c t)"), "SG",
                    waits=[("SP", sp_f2.get((p, f2c - 2),
                                            sp_f2.get((p - 1, 30 + f2c), 0)))])
                for f1c in range(34):
                    w = []
                    if f1c == 0:
                        w = [("SG", sg_k1),
                             ("SA", sa_r2.get((p, f2c - 2), 0))]
                        if ffn2_serial:
                            w.append(("SA", sa_r2.get((p, f2c - 1), 0)))
                            w.append(("SG", sch.cnt["SG"]))
                            w.append(("SX", sch.cnt["SX"]))
                    if f1c == 32:
                        w = [("SA", sa_rs1), ("SV", sv_rs1)]
                    if f1c < 32:
                        sp_f2[(p, f2c)] = sch.op(
                            "tensor",
                            lambda e, sl=sl, f1c=f1c: e.matmul(
                                psF[:, sl * 512:(sl + 1) * 512],
                                k1b[:, sl * 4096 + f1c * 128:
                                    sl * 4096 + (f1c + 1) * 128],
                                h1[:, f1c * 512:(f1c + 1) * 512],
                                start=(f1c == 0),
                                stop=(ffn2_nocorr and f1c == 31)),
                            inc="SP" if (ffn2_nocorr and f1c == 31) else None,
                            waits=w)
                    elif not ffn2_nocorr:
                        rhs = rs1_hi if f1c == 32 else rs1_lo
                        sp_f2[(p, f2c)] = sch.op(
                            "tensor",
                            lambda e, sl=sl, rhs=rhs, f1c=f1c: e.matmul(
                                psF[:, sl * 512:(sl + 1) * 512],
                                negh_bf[0:1, 0:128], rhs[:],
                                start=False, stop=(f1c == 33)),
                            inc="SP" if f1c == 33 else None, waits=w)
                sa_r2[(p, f2c)] = sch.op(
                    "scalar",
                    lambda e, sl=sl, f2c=f2c: e.activation(
                        h2[:, f2c * 512:(f2c + 1) * 512],
                        psF[:, sl * 512:(sl + 1) * 512], AF.Relu,
                        scale=scl[:, 1:2]),
                    inc="SA", waits=[("SP", sp_f2[(p, f2c)])])
                for rc in ([f2c - 1] if f2c >= 1 else []):
                    w = [("SA", sa_r2.get((p, rc + 1), sa_r2[(p, rc)]))]
                    if rc == 0:
                        w.append(("SA", sa_rs2))
                    sch.op(
                        "tensor",
                        lambda e, rc=rc: e.matmul(
                            psRS[0:1, 0:512], ones_pe[:],
                            h2[:, rc * 512:(rc + 1) * 512],
                            start=(rc == 0), stop=False),
                        inc=None, waits=w)
            if ffn_stage_barriers:
                sch.phase_barrier()
            fa = sch.op("scalar",
                        lambda e: e.copy(fen_a[:],
                                         h2[:, 31 * 512:31 * 512 + 8]),
                        inc="SA", waits=[("SA", sa_r2[(p, 31)])])
            sp_rs2 = sch.op(
                "tensor",
                lambda e: e.matmul(
                    psRS[0:1, 0:512], ones_pe[:],
                    h2[:, 31 * 512:32 * 512],
                    start=False, stop=True),
                inc="SP", waits=[("SA", fa)])
            sa_rs2 = sch.op("scalar",
                            lambda e: e.copy(rs2_sb[:], psRS[0:1, 0:512]),
                            inc="SA", waits=[("SP", sp_rs2)])
            sa_rs2 = sch.op("scalar",
                            lambda e: e.copy(rs2_hi[:], rs2_sb[:]),
                            inc="SA", waits=[("SA", sa_rs2)])
            sa_rs2 = sch.op("scalar",
                            lambda e: e.copy(fen_a[0:1, :], rs2_hi[0:1, 0:8]),
                            inc="SA", waits=[("SA", sa_rs2)])
            sv_rs2 = sch.op("vector",
                            lambda e: e.tensor_sub(rs2_lo[:], rs2_sb[:],
                                                   rs2_hi[:]),
                            inc="SV", waits=[("SA", sa_rs2)])
            sv_rs2 = sch.op("vector",
                            lambda e: e.tensor_copy(fen_v[0:1, :],
                                                    rs2_lo[0:1, 0:8]),
                            inc="SV", waits=[("SV", sv_rs2)])
            if tap == "h1" and p == 0:
                sch.wait("sync", ("SA", sa_r1[(0, 31)]))
                sch.dma("sync", tap_ap[:].rearrange("p c t -> p (c t)"),
                        h1[:], "SO")
            if tap == "h2" and p == 0:
                sch.wait("sync", ("SA", sa_fl))
                sch.dma("sync", tap_ap[:].rearrange("p c t -> p (c t)"),
                        h2[:], "SO")
            if ffn_stage_barriers:
                sch.phase_barrier()
            for tc in range(4):
                sl = tc % 2
                sx_ac3 = sch.dma(
                    "sync", at[:, sl * 1024:(sl + 1) * 1024],
                    accs[p * 512 + tc * 128: p * 512 + (tc + 1) * 128, :], "SX",
                    waits=[("SX", sx_acw[15] if p == 0 and tc == 0 else 0),
                           ("SV", sv_f3.get((p, tc - 2, 1), 0)),
                           ("SO", so_out)])
                for fh in range(2):
                    for f2c in range(34):
                        w = []
                        if f2c == 0:
                            w = [("SG", sg_w2),
                                 ("SV", sv_f3.get((p, tc - 1, fh), 0)
                                  if tc >= 1 else sv_f3.get((p - 1, 3, fh), 0))]
                        if f2c == 32:
                            w = [("SA", sa_rs2), ("SV", sv_rs2)]
                        if f2c < 32:
                            sp_f3[(p, tc, fh)] = sch.op(
                                "tensor",
                                lambda e, fh=fh, f2c=f2c, tc=tc: e.matmul(
                                    psF3[:, fh * 512:(fh + 1) * 512],
                                    h2[:, f2c * 512 + tc * 128:
                                       f2c * 512 + (tc + 1) * 128],
                                    W2sb[:, f2c * 1024 + fh * 512:
                                         f2c * 1024 + (fh + 1) * 512],
                                    start=(f2c == 0), stop=False),
                                inc=None, waits=w)
                        else:
                            lhs = rs2_hi if f2c == 32 else rs2_lo
                            sp_f3[(p, tc, fh)] = sch.op(
                                "tensor",
                                lambda e, fh=fh, tc=tc, lhs=lhs, f2c=f2c:
                                e.matmul(
                                    psF3[:, fh * 512:(fh + 1) * 512],
                                    lhs[0:1, tc * 128:(tc + 1) * 128],
                                    negh_bf[0:1, 0:512],
                                    start=False, stop=(f2c == 33)),
                                inc="SP" if f2c == 33 else None, waits=w)
                    sv_f3[(p, tc, fh)] = sch.op(
                        "vector",
                        lambda e, sl=sl, fh=fh: e.scalar_tensor_tensor(
                            ob[:, sl * 1024 + fh * 512:
                               sl * 1024 + (fh + 1) * 512],
                            psF3[:, fh * 512:(fh + 1) * 512], scl[:, 2:3],
                            at[:, sl * 1024 + fh * 512:
                               sl * 1024 + (fh + 1) * 512],
                            ALU.mult, ALU.add),
                        inc="SV",
                        waits=[("SP", sp_f3[(p, tc, fh)]), ("SX", sx_ac3)])
                sv_fen = sch.op(
                    "vector", lambda e: e.tensor_copy(fen_v[:], ob[:, 0:8]),
                    inc="SV", waits=[("SV", sv_f3[(p, tc, 1)])])
                so_out = sch.dma(
                    "sync",
                    out[p * 512 + tc * 128: p * 512 + (tc + 1) * 128, :],
                    ob[:, sl * 1024:(sl + 1) * 1024], "SO",
                    waits=[("SV", sv_fen)])
            prev_pass_sp = sch.cnt["SP"]

        sch.wait("sync", ("SO", sch.cnt["SO"]))
        sch.flush(block)
        pc.close()

    stack.close()
    return nc



def _build_abias(role):
    MSK = -30000.0
    ab = np.full((5, NDT, 128, 128), MSK, np.float32)
    ik = np.arange(128)[:, None]
    jq = np.arange(128)[None, :]
    for c in range(5):
        for dt in range(NDT):
            delta = jq - ik + 512 - 128 * dt
            valid = (delta >= 0) & (delta <= 512)
            if role == 0 and c < 4 and (c + dt) < 4:
                valid = np.zeros_like(valid)
            if role == 1 or c == 4:
                m = 512.0
            else:
                m = np.minimum(512.0, 128.0 * c + jq)
            ab[c, dt] = np.where(valid, delta - m, MSK)
    return ab.astype(np.float16)


def make_core_inputs(x, w_in, w_out, k0, k1, k2, s0, s1, s2):
    x = np.asarray(x)
    w_in = np.ascontiguousarray(np.asarray(w_in))
    w_out = np.ascontiguousarray(np.asarray(w_out))
    k0 = np.asarray(k0).astype(np.uint8)
    k1 = np.asarray(k1).astype(np.uint8)
    k2 = np.asarray(k2).astype(np.uint8)

    w_inT = np.ascontiguousarray(w_in.T)
    wg = np.empty((4, 8, 128, 768), np.float32)
    for g in range(4):
        for ec in range(8):
            rows = w_inT[ec * 128:(ec + 1) * 128]
            wg[g, ec, :, 0:256] = rows[:, 256 * g: 256 * g + 256]
            wg[g, ec, :, 256:512] = rows[:, 1024 + 256 * g: 1024 + 256 * g + 256]
            wg[g, ec, :, 512:768] = rows[:, 2048 + 256 * g: 2048 + 256 * g + 256]
    wo = np.ascontiguousarray(w_out.T.reshape(8, 128, 1024))
    k0r = np.ascontiguousarray(k0.reshape(16, 128, F1))
    k1r = np.ascontiguousarray(
        k1.reshape(32, 128, 32, 128).transpose(2, 1, 0, 3))
    k2r = np.ascontiguousarray(k2.reshape(32, 128, 1024))
    svec = np.ascontiguousarray(
        np.broadcast_to(
            np.array([np.float32(s0), np.float32(s1), np.float32(s2)],
                     np.float32)[None, :], (128, 3)))
    eye = np.eye(128, dtype=ml_dtypes.bfloat16)
    ab = [_build_abias(0), _build_abias(1)]

    ins = []
    for c in range(8):
        b, role = c // 2, c % 2
        xcr = np.empty((TOKH, E), np.float32)
        if role == 0:
            xcr[:HALO] = 0.0
            xcr[HALO:] = x[b, 0:TOK]
        else:
            xcr[:HALO] = x[b, TOK - HALO:TOK]
            xcr[HALO:] = x[b, TOK:]
        ins.append(dict(xc=np.ascontiguousarray(xcr), wg=wg, wo=wo, k0=k0r,
                        k1r=k1r, k2r=k2r, svec=svec, abias=ab[role], eye=eye))
    return ins


def kernel(L0_attn_x, L0_attn_w_in, L0_attn_w_out, L0_ffn_k0, L0_ffn_k1,
           L0_ffn_k2, L0_ffn_s0, L0_ffn_s1, L0_ffn_s2):
    ins = make_core_inputs(L0_attn_x, L0_attn_w_in, L0_attn_w_out, L0_ffn_k0,
                           L0_ffn_k1, L0_ffn_k2, L0_ffn_s0, L0_ffn_s1,
                           L0_ffn_s2)
    nc = build_program()
    res = run_bass_kernel_spmd(nc, ins, core_ids=list(range(8)))
    out = np.empty((B, S, E), np.float32)
    for c in range(8):
        b, role = c // 2, c % 2
        out[b, role * TOK:(role + 1) * TOK] = res.results[c]["out"]
    return out
